# version 14
# baseline (speedup 1.0000x reference)
import numpy as np
import ml_dtypes
from contextlib import ExitStack
import concourse.bass as bass
import concourse.mybir as mybir
from concourse.bass_utils import run_bass_kernel_spmd

F32 = mybir.dt.float32
BF16 = mybir.dt.bfloat16
U8 = mybir.dt.uint8
AF = mybir.ActivationFunctionType
AX = mybir.AxisListType
ALU = mybir.AluOpType

H = 2048
S_TOT = 8192
NCORE = 8
T = 1024
NT = 8
NKC = 16
NQH = 16
NKV = 8
FFI = 8192
NIC = 64
EPS = 1e-5
ISQ = float(1.0 / np.sqrt(128.0))
NEG = -1.0e30
NKEY = 139

BASE = 16512
SB_TOP = 229376

ENGINES = ("sync", "gpsimd", "tensor", "vector", "scalar")
COMPUTE = ("tensor", "vector", "scalar")


class Sched:
    DEDUP = True

    def __init__(self):
        self.ops = {e: [] for e in ENGINES}
        self.cnt = {e: 0 for e in ENGINES}
        self.n = 0

    def add(self, eng, fn, waits=(), inc=False, inc_by=1):
        tok = None
        if inc:
            self.cnt[eng] += inc_by
            tok = (eng, self.cnt[eng])
        ws = [(s, v) for (s, v) in waits if v > 0]
        self.ops[eng].append((ws, fn, self.cnt[eng] if inc else None, inc_by))
        self.n += 1
        return tok

    def emit(self, nc, sems):
        import os as _os
        simmode = _os.environ.get("BASS_SIM_FP64") == "1"
        with nc.Block() as block:
            for eng in ENGINES:
                ops = self.ops[eng]
                if not ops:
                    continue

                def body(e, ops=ops, eng=eng):
                    hi = {k: 0 for k in ENGINES}
                    own = 0
                    for ws, fn, inc_to, inc_by in ops:
                        if simmode and own > hi[eng]:
                            e.wait_ge(sems[eng], own)
                            hi[eng] = own
                        for s, v in ws:
                            if s == eng and not simmode:
                                continue
                            if v > hi[s] or not Sched.DEDUP:
                                e.wait_ge(sems[s], v)
                                hi[s] = max(hi[s], v)
                        ins = fn(e)
                        if inc_to is not None:
                            ins.then_inc(sems[eng], inc_by)
                            own = inc_to

                getattr(block, eng)(body)


def later(*toks):
    best = {}
    for t in toks:
        if t is None:
            continue
        if isinstance(t, list):
            for s, v in t:
                if v > best.get(s, -1):
                    best[s] = v
            continue
        s, v = t
        if v > best.get(s, -1):
            best[s] = v
    return [(s, v) for s, v in best.items()]


def build_nc():
    nc = bass.Bass("TRN2", target_bir_lowering=False)

    d_x = nc.dram_tensor("x", [T, H], F32, kind="ExternalInput")
    d_xh = nc.dram_tensor("xh", [11, H], F32, kind="ExternalInput")
    d_wq = nc.dram_tensor("wq", [16, 128, 2048], F32, kind="ExternalInput")
    d_wk = nc.dram_tensor("wk", [8, 128, 2048], F32, kind="ExternalInput")
    d_wv = nc.dram_tensor("wv", [8, 128, 2048], F32, kind="ExternalInput")
    d_wo = nc.dram_tensor("wo", [16, 128, 2048], F32, kind="ExternalInput")
    d_wg = nc.dram_tensor("wg8", [64, 128, 2048], U8, kind="ExternalInput")
    d_wu = nc.dram_tensor("wu8", [64, 128, 2048], U8, kind="ExternalInput")
    d_wd = nc.dram_tensor("wd8", [16, 128, 8192], U8, kind="ExternalInput")
    d_gs = nc.dram_tensor("gs", [128, 64], F32, kind="ExternalInput")
    d_us = nc.dram_tensor("us", [128, 64], F32, kind="ExternalInput")
    d_ds = nc.dram_tensor("ds", [128, 16], F32, kind="ExternalInput")
    d_m0 = nc.dram_tensor("mask0", [128, NKEY], F32, kind="ExternalInput")
    d_mm = nc.dram_tensor("maskm", [128, NKEY], F32, kind="ExternalInput")
    d_out = nc.dram_tensor("out", [T, H], F32, kind="ExternalOutput")
    d_xp = nc.dram_tensor("xp", [T, H], F32, kind="Internal")

    ident_np = np.eye(128, dtype=np.float32)
    d_id32 = nc.inline_tensor(ident_np, "id32")
    d_idbf = nc.inline_tensor(ident_np.astype(ml_dtypes.bfloat16), "idbf")

    def at(name, shape, dtype, off):
        end = off + int(np.prod(shape[1:])) * mybir.dt.size(dtype)
        assert BASE <= off and end <= SB_TOP, (name, off, end)
        return nc.alloc_sbuf_tensor_at(name, shape, dtype, offset=off)

    P0 = BASE
    sb_m0 = at("sb_m0", [128, NKEY], F32, P0 + 0)
    sb_mm = at("sb_mm", [128, NKEY], F32, P0 + 576)
    sb_idbf = at("sb_idbf", [128, 128], BF16, P0 + 1152)
    sb_id32 = at("sb_id32", [128, 128], F32, P0 + 1408)
    sb_gs = at("sb_gs", [128, 64], F32, P0 + 1920)
    sb_us = at("sb_us", [128, 64], F32, P0 + 2176)
    sb_ds = at("sb_ds", [128, 16], F32, P0 + 2432)
    sb_ones = at("sb_ones", [128, 2], BF16, P0 + 2496)
    sb_n128 = at("sb_n128", [1, 128], BF16, P0 + 2560)
    def statpair(name, off):
        return [at(name + str(i), [128, 1], F32, off + 32 * i) for i in range(2)]

    sb_nsum = statpair("sb_nsum", P0 + 2816)
    sb_nm = statpair("sb_nm", P0 + 2880)
    sb_v2k = statpair("sb_v2k", P0 + 2944)
    sb_lnv = statpair("sb_lnv", P0 + 3008)
    sb_rstd = statpair("sb_rstd", P0 + 3072)
    sb_nmr = statpair("sb_nmr", P0 + 3136)

    sb_zero = at("sb_zero", [128, 1], F32, P0 + 3520)
    sb_eps = at("sb_eps", [128, 1], F32, P0 + 3552)
    sb_csum = at("sb_csum", [1, 1024], BF16, P0 + 3584)
    sb_hsum = at("sb_hsum", [1, 1024], BF16, P0 + 5632)
    sb_sq = at("sb_sq", [128, 2048], BF16, P0 + 7680)
    sb_xh = at("sb_xh", [11, H], F32, P0 + 11776)
    sb_sqB = at("sb_sqB", [128, 2048], BF16, P0 + 11776)

    A0 = P0 + 19968
    sb_x = [at(f"sb_x{t}", [128, H], F32, A0 + t * 8192) for t in range(NT)]

    B0 = A0 + 65536
    sb_ln1T = [at(f"sb_ln1T{c}", [128, 1040], BF16, B0 + c * 2080) for c in range(NKC)]
    QF0 = B0 + 33280
    sb_q = [at(f"sb_q{h}", [128, 1024], BF16, QF0 + h * 2048) for h in range(NQH)]
    KF0 = QF0 + 32768
    sb_k = [at(f"sb_k{h}", [128, 1040], BF16, KF0 + h * 2080) for h in range(NKV)]
    VF0 = KF0 + 16640
    sb_v = [at(f"sb_v{h}", [128, 1040], BF16, VF0 + h * 2080) for h in range(NKV)]

    C0 = VF0 + 16640
    SL0, SL1, SL2 = C0, C0 + 8192, C0 + 16384
    sb_wb = [at(f"sb_wb{i}", [128, 2048], BF16, SL0 + i * 4096) for i in range(2)]
    sb_wgb = [at(f"sb_wgb{i}", [128, 2048], BF16, SL0 + i * 4096) for i in range(2)]
    sb_dtm2 = [at("sb_dtm", [128, 2048], F32, SL0),
               at("sb_dtmB", [128, 2048], F32, P0 + 11776)]
    sb_lnbf = [at(f"sb_lnbf{i}", [128, 2048], BF16, SL1 + i * 4096) for i in range(2)]
    AT0 = SL1
    sb_ssb = [at(f"sb_ssb{i}", [128, NKEY], F32, AT0 + i * 576) for i in range(4)]
    sb_psb = [at(f"sb_psb{i}", [128, NKEY], F32, AT0 + 2304 + i * 576) for i in range(8)]
    sb_abf = [at(f"sb_abf{i}", [128, NKEY], BF16, AT0 + 6912 + i * 288) for i in range(4)]
    sb_at1 = [at(f"sb_at1{i}", [128, 128], BF16, AT0 + 8064 + i * 256) for i in range(4)]
    sb_at2 = [at(f"sb_at2{i}", [11, 128], BF16, AT0 + 9088 + i * 256) for i in range(4)]
    sb_ngmx = [at(f"sb_ngmx{i}", [128, 1], F32, AT0 + 10112 + i * 32) for i in range(4)]
    sb_sume = [at(f"sb_sume{i}", [128, 1], F32, AT0 + 10240 + i * 32) for i in range(4)]
    sb_rsum = [at(f"sb_rsum{i}", [128, 1], F32, AT0 + 10368 + i * 32) for i in range(8)]
    sb_kscr = [at(f"sb_kscr{i}", [128, NKEY], BF16, AT0 + 10624 + i * 288) for i in range(8)]
    sb_vscr = [at(f"sb_vscr{i}", [128, NKEY], BF16, AT0 + 12928 + i * 288) for i in range(8)]
    sb_vt1 = [at(f"sb_vt1{i}", [128, 128], BF16, AT0 + 15232 + i * 256) for i in range(8)]
    sb_vt2 = [at(f"sb_vt2{i}", [11, 128], BF16, AT0 + 17280 + i * 256) for i in range(8)]
    sb_trash = at("sb_trash", [128, 1], F32, AT0 + 19328)
    sb_wub = [at(f"sb_wub{i}", [128, 2048], BF16, SL1 + i * 4096) for i in range(2)]
    sb_xpr0 = at("sb_xpr0", [128, 2048], F32, SL1)
    sb_otm2 = [at("sb_otm", [128, 2048], F32, SL2),
               at("sb_otmB", [128, 2048], F32, SL1 + 10624)]
    sb_uact = at("sb_uact", [128, 1024], F32, SL2)
    sb_sact = at("sb_sact", [128, 1024], F32, SL2 + 4096)
    sb_xpr1 = at("sb_xpr1", [128, 2048], F32, SL2)
    sb_xpr = [sb_xpr0, sb_xpr1]
    sb_sqC = at("sb_sqC", [128, 2048], BF16, SL2)
    sb_ao = [at(f"sb_ao{h}", [128, 1024], BF16, B0 + h * 2048) for h in range(NQH)]
    sb_ofm = ([at(f"sb_ofm{i}", [128, 1024], F32, QF0 + i * 4096) for i in range(8)]
              + [at(f"sb_ofm{i + 8}", [128, 1024], F32, KF0 + i * 4096) for i in range(8)])
    sb_ln3T = [at(f"sb_ln3T{c}", [128, 1024], BF16, B0 + c * 2048) for c in range(NKC)]
    sb_h = ([at(f"sb_h{i}", [128, 1024], BF16, A0 + i * 2048) for i in range(32)]
            + [at(f"sb_h{i + 32}", [128, 1024], BF16, QF0 + i * 2048) for i in range(16)]
            + [at(f"sb_h{i + 48}", [128, 1024], BF16, KF0 + i * 2048) for i in range(16)])
    sb_wdb = [at(f"sb_wdb{i}", [128, 4096], BF16, B0 + i * 8192) for i in range(2)]
    sb_dq = [at(f"sb_dq{i}", [128, 256], F32, B0 + 16384 + i * 1024) for i in range(16)]

    S = Sched()
    pst = ExitStack()

    t_ones = S.add("vector", lambda e: e.memset(sb_ones[:], 1.0), inc=True)
    t_n128 = S.add("vector", lambda e: e.memset(sb_n128[:], -128.0), inc=True)
    t_zero = S.add("vector", lambda e: e.memset(sb_zero[:], 0.0), inc=True)
    t_eps = S.add("vector", lambda e: e.memset(sb_eps[:], EPS), inc=True)
    for _par in range(2):
        tw1 = S.add("vector", lambda e, o=sb_ngmx[_par][:, :]:
                    e.reduce_max(o, sb_eps[:, :], axis=AX.X, negate=True),
                    waits=later(t_eps), inc=True)
        tw2 = S.add("scalar", lambda e, o=sb_sume[_par][:, :], b=sb_ngmx[_par][:, :]:
                    e.activation(o, sb_eps[:, :], AF.Exp, bias=b, scale=1.0),
                    waits=later(tw1), inc=True)
        tw3 = S.add("vector", lambda e, o=sb_rsum[_par][:, :], i=sb_sume[_par][:, :]:
                    e.reciprocal(o, i), waits=later(tw2), inc=True)
        S.add("scalar", lambda e, o=sb_lnv[_par][:, :], s=sb_rsum[_par][:, :]:
              e.activation(o, sb_eps[:, :], AF.Copy, bias=0.0, scale=s),
              waits=later(tw3), inc=True)
    t_pm = S.add("vector", lambda e: e.memset(sb_sq[:, :], 1.0), waits=later(t_eps), inc=True)
    tw = t_pm
    for _r in range(6):
        ts1 = S.add("scalar", lambda e: e.activation(sb_sq[:, 0:2048], sb_sq[:, 0:2048],
                                                     AF.Square, bias=sb_zero[:, :], scale=1.0),
                    waits=later(tw), inc=True)
        ts4 = S.add("scalar", lambda e: e.activation(sb_sq[:, 0:512], sb_sq[:, 0:512],
                                                     AF.Silu, bias=sb_zero[:, :], scale=1.0),
                    waits=later(ts1), inc=True)
        ts5 = S.add("scalar", lambda e: e.activation(sb_sq[:, 0:512], sb_sq[:, 0:512],
                                                     AF.Identity, bias=sb_zero[:, :], scale=1.0),
                    waits=later(ts4), inc=True)
        tv1 = S.add("vector", lambda e, o=sb_v2k[_r % 2][:, :]:
                    e.reduce_sum(o, sb_sq[:, :], axis=AX.X), waits=later(ts5), inc=True)
        tv2 = S.add("vector", lambda e, o=sb_ngmx[_r % 2][:, :]:
                    e.reduce_max(o, sb_sq[:, :], axis=AX.X, negate=True),
                    waits=later(tv1), inc=True)
        ts2 = S.add("scalar", lambda e, i=sb_v2k[_r % 2][:, :], o=sb_lnv[_r % 2][:, :]:
                    e.activation(o, i, AF.Ln, bias=sb_eps[:, :], scale=1.0 / H),
                    waits=later(tv1), inc=True)
        ts3 = S.add("scalar", lambda e, i=sb_lnv[_r % 2][:, :], o=sb_rstd[_r % 2][:, :]:
                    e.activation(o, i, AF.Exp, bias=sb_zero[:, :], scale=-0.5),
                    waits=later(ts2), inc=True)
        tv3 = S.add("vector", lambda e, o=sb_rsum[_r % 2][:, :], i=sb_rstd[_r % 2][:, :]:
                    e.reciprocal(o, i), waits=later(ts3), inc=True)
        tv4 = S.add("vector", lambda e, o=sb_nmr[_r % 2][:, :], a=sb_rsum[_r % 2][:, :],
                    b=sb_rstd[_r % 2][:, :]: e.tensor_mul(o, a, b),
                    waits=later(tv3), inc=True)
        tw = later(ts5, tv4, tv2)
    tb1 = S.add("gpsimd", lambda e: e.dma_start(out=d_xp[0:64, :], in_=sb_sq[0:64, :]),
                waits=later(tw), inc=True, inc_by=16)
    tb2 = S.add("gpsimd", lambda e: e.dma_start(out=d_xp[64:128, :], in_=sb_sq[64:128, :]),
                inc=True, inc_by=16)
    t_go = later(tb1, tb2, tw)
    t_zero = later(t_zero, t_go)
    t_eps = later(t_eps, t_go)

    def dma(eng, dst, src, waits=(), inc=True):
        return S.add(eng, lambda e, d=dst, s=src: e.dma_start(out=d, in_=s),
                     waits=waits, inc=inc, inc_by=16)

    t_m0 = dma("sync", sb_m0[:], d_m0[:, :])
    t_mm = dma("sync", sb_mm[:], d_mm[:, :])
    t_idbf = dma("sync", sb_idbf[:], d_idbf[:, :])
    t_id32 = dma("sync", sb_id32[:], d_id32[:, :])
    t_gsl = dma("sync", sb_gs[:], d_gs[:, :])
    t_usl = dma("sync", sb_us[:], d_us[:, :])
    t_dsl = dma("sync", sb_ds[:], d_ds[:, :])
    t_x = [dma("sync", sb_x[t][:], d_x[t * 128:(t + 1) * 128, :]) for t in range(NT)]
    t_xh = dma("sync", sb_xh[:], d_xh[:, :])

    sq_free = {"t": None}

    def ln_tile(x_ap, P, par, dep, out_bf_ap, sqt=None, sqf=None):
        sqt = sb_sq if sqt is None else sqt
        sqf = sq_free if sqf is None else sqf
        a_nsum, a_nm, a_v2k = sb_nsum[par][0:P, :], sb_nm[par][0:P, :], sb_v2k[par][0:P, :]
        a_lnv, a_rstd, a_nmr = sb_lnv[par][0:P, :], sb_rstd[par][0:P, :], sb_nmr[par][0:P, :]
        t0 = S.add("vector", lambda e, x=x_ap, o=a_nsum: e.reduce_sum(o, x, axis=AX.X, negate=True),
                   waits=later(dep), inc=True)
        tm = S.add("scalar", lambda e, o=a_nm, i=a_nsum: e.mul(o, i, 1.0 / H),
                   waits=later(t0), inc=True)
        tb = S.add("vector", lambda e, o=a_nsum, i=a_nm: e.tensor_copy(o, i),
                   waits=later(tm), inc=True)
        tsq = S.add("scalar", lambda e, x=x_ap, b=a_nm, sq=sqt[0:P, :]:
                    e.activation(sq, x, AF.Square, bias=b, scale=1.0),
                    waits=later(sqf["t"], tb), inc=True)
        tvv = S.add("vector", lambda e, o=a_v2k, i=sqt[0:P, :]:
                    e.reduce_sum(o, i, axis=AX.X),
                    waits=later(tsq), inc=True)
        sqf["t"] = tvv
        S.add("scalar", lambda e, o=a_lnv, i=a_v2k, b=sb_eps[0:P, :]:
              e.activation(o, i, AF.Ln, bias=b, scale=1.0 / H),
              waits=later(t_eps, tvv), inc=True)
        t4 = S.add("scalar", lambda e, o=a_rstd, i=a_lnv, b=sb_zero[0:P, :]:
                   e.activation(o, i, AF.Exp, bias=b, scale=-0.5),
                   waits=later(t_zero), inc=True)
        t5 = S.add("vector", lambda e, o=a_nmr, a=a_nm, b=a_rstd: e.tensor_mul(o, a, b),
                   waits=later(t4), inc=True)
        t6 = S.add("scalar", lambda e, o=out_bf_ap, x=x_ap, b=a_nmr, s=a_rstd:
                   e.activation(o, x, AF.Identity, bias=b, scale=s),
                   waits=later(t5), inc=True)
        return t6

    p_tbf = [pst.enter_context(nc.psum_tensor(f"p_tbf{i}", [128, 128], BF16)) for i in range(2)]
    t_ptbf = [None, None]
    t_ln1T_done = []
    t_lnbf_free = [None, None]
    pp_ctr = [0]

    sq_freeC = {"t": None}
    sq1 = [(sb_sq, sq_free), (sb_sqC, sq_freeC)]

    def ln1_iter(tt):
        par = tt % 2
        P = 128 if tt < NT else 11
        x_ap = sb_x[tt][:, :] if tt < NT else sb_xh[:, :]
        dep = later(t_x[tt] if tt < NT else t_xh,
                    (t_x[tt + 1] if tt + 1 < NT else t_xh), t_lnbf_free[par], t_go)
        t_ln = ln_tile(x_ap, P, par, dep, sb_lnbf[par][0:P, :],
                       sqt=sq1[par][0], sqf=sq1[par][1])
        col = tt * 128 if tt < NT else 1024
        for fc in range(NKC):
            pp = pp_ctr[0] % 2
            pp_ctr[0] += 1
            tmm = S.add("tensor",
                        lambda e, o=p_tbf[pp][0:128, 0:P], i=sb_lnbf[par][0:P, fc * 128:(fc + 1) * 128],
                        idn=sb_idbf[0:P, 0:P]:
                        e.transpose(o, i, idn),
                        waits=later(t_ln, t_ptbf[pp], t_idbf), inc=True)
            tev = S.add("vector",
                        lambda e, d=sb_ln1T[fc][:, col:col + P], s=p_tbf[pp][0:128, 0:P]:
                        e.tensor_copy(d, s),
                        waits=later(tmm), inc=True)
            t_ptbf[pp] = tev
            if fc == NKC - 1:
                t_lnbf_free[par] = tmm
            t_ln1T_done.append(tev)

    for tt in range(NT + 1):
        ln1_iter(tt)
    ln1_iter(0)
    ln1_iter(1)
    t_ln1T_all = later(*t_ln1T_done)

    p_mm = [pst.enter_context(nc.psum_tensor(f"p_mm{i}", [128, 512], F32)) for i in range(3)]
    p_mh = pst.enter_context(nc.psum_tensor("p_mh", [128, 16], F32))
    t_pmm = [None, None, None]
    t_pmh = None
    wsrc = [(d_wq, oc, sb_q[oc], 2) for oc in range(16)] + \
           [(d_wk, oc, sb_k[oc], 3) for oc in range(8)] + \
           [(d_wv, oc, sb_v[oc], 3) for oc in range(8)]
    t_wbuf_free = [None, None]
    t_wload = {}

    def wload(i, idx):
        dten, oc, _, _ = wsrc[idx]
        t_wload[idx] = S.add("gpsimd",
                             lambda e, d=sb_wb[i][:, :], s=dten[oc, :, :]: e.dma_start(out=d, in_=s),
                             waits=later(t_wbuf_free[i]), inc=True, inc_by=16)

    wload(0, 0)
    wload(1, 1)
    bank = 0
    for idx, (dten, oc, dst, nseg) in enumerate(wsrc):
        wb = idx % 2
        t_w = t_wload[idx]
        t_grp = None
        for seg in range(nseg):
            if seg < 2:
                ps_ap = p_mm[bank % 3][:, 0:512]
                t_free = t_pmm[bank % 3]
                ncol, rc0 = 512, seg * 512
            else:
                ps_ap = p_mh[:, 0:11]
                t_free = t_pmh
                ncol, rc0 = 11, 1024
            for kc in range(NKC):
                w = later(t_w, t_ln1T_all, t_free) if kc == 0 else ()
                S.add("tensor",
                      lambda e, o=ps_ap, l=sb_wb[wb][:, kc * 128:(kc + 1) * 128],
                      r=sb_ln1T[kc][:, rc0:rc0 + ncol], k=kc:
                      e.matmul(o, l, r, start=(k == 0), stop=(k == NKC - 1)),
                      waits=w, inc=(kc == NKC - 1))
            t_grp = ("tensor", S.cnt["tensor"])
            tev = S.add("vector",
                        lambda e, d=dst[:, rc0:rc0 + ncol], s=ps_ap: e.tensor_copy(d, s),
                        waits=later(t_grp), inc=True)
            if seg < 2:
                t_pmm[bank % 3] = tev
                bank += 1
            else:
                t_pmh = tev
        t_wbuf_free[wb] = t_grp
        if idx + 2 < len(wsrc):
            wload(wb, idx + 2)
    t_wb_free_qkv = list(t_wbuf_free)
    t_qkv_all = later(("vector", S.cnt["vector"]), ("tensor", S.cnt["tensor"]))

    pst.close()
    pst = ExitStack()
    p_ss = [pst.enter_context(nc.psum_tensor(f"p_ss{i}", [128, NKEY], F32)) for i in range(2)]
    p_av = [pst.enter_context(nc.psum_tensor(f"p_av{i}", [128, 128], F32)) for i in range(2)]
    p_vt = [pst.enter_context(nc.psum_tensor(f"p_vt{i}", [128, 128], BF16)) for i in range(2)]
    p_at = [pst.enter_context(nc.psum_tensor(f"p_at{i}", [128, 128], BF16)) for i in range(2)]
    t_pss = [None, None]
    t_pav = [None, None]
    t_pvt = [None, None]
    t_pat = [None, None]
    t_attn_out = []

    iters = [(tt, kv) for tt in range(NT) for kv in range(NKV)] + [(0, 0), (0, 1)]
    NIT = len(iters)
    W = 2 * NIT
    tok_scr = {}
    tok_scr_free = {}
    tok = {k: {} for k in ("sc", "ms", "mx", "ex", "sm", "ln", "rc", "nr", "at", "av")}

    def stage_A(j):
        if j >= NIT:
            return
        tt, kv = iters[j]
        ip = j % 8
        if tt == 0:
            copies = [(0, 1031, 4), (4, 1024, 7), (11, 0, 128)]
        else:
            copies = [(0, 1031, 4), (4, tt * 128 - 7, 7), (11, tt * 128, 128)]
        dep0 = later(t_qkv_all, tok_scr_free.get(j - 8), t_pvt[j % 2])
        tk = tv = None
        for (dcol, scol, n) in copies:
            tk = S.add("vector",
                       lambda e, d=sb_kscr[ip][:, dcol:dcol + n], s=sb_k[kv][:, scol:scol + n]:
                       e.tensor_copy(d, s), waits=dep0, inc=True)
            tv = S.add("vector",
                       lambda e, d=sb_vscr[ip][:, dcol:dcol + n], s=sb_v[kv][:, scol:scol + n]:
                       e.tensor_copy(d, s), inc=True)
            dep0 = ()
        t_scr = later(tk, tv)
        tv1 = S.add("tensor", lambda e, o=p_vt[j % 2][:, :], i=sb_vscr[ip][:, 11:139]:
                    e.transpose(o, i, sb_idbf[:, :]), waits=later(t_scr), inc=True)
        ev1 = S.add("vector", lambda e, d=sb_vt1[ip][:, :], s=p_vt[j % 2][:, :]:
                    e.tensor_copy(d, s), waits=later(tv1), inc=True)
        tv2 = S.add("tensor", lambda e, o=p_vt[j % 2][0:11, :], i=sb_vscr[ip][:, 0:11]:
                    e.transpose(o, i, sb_idbf[:, :]), waits=later(ev1), inc=True)
        ev2 = S.add("vector", lambda e, d=sb_vt2[ip][:, :], s=p_vt[j % 2][0:11, :]:
                    e.tensor_copy(d, s), waits=later(tv2), inc=True)
        t_pvt[j % 2] = ev2
        tok_scr[j] = later(t_scr, ev1, ev2)

    def st_Q(w):
        j, qi = w // 2, w % 2
        tt, kv = iters[j]
        qh = 2 * kv + qi
        tok["sc"][w] = S.add("tensor",
                             lambda e, o=p_ss[w % 2][:, :], l=sb_q[qh][:, tt * 128:(tt + 1) * 128],
                             r=sb_kscr[j % 8][:, :]: e.matmul(o, l, r, start=True, stop=True),
                             waits=later(tok_scr[j], tok["ms"].get(w - 2)), inc=True)

    def st_M(w):
        j = w // 2
        tt, kv = iters[j]
        mask = sb_m0 if tt == 0 else sb_mm
        u4 = w % 4
        tok["ms"][w] = S.add("vector",
                             lambda e, o=sb_ssb[u4][:, :], i=p_ss[w % 2][:, :], m=mask[:, :]:
                             e.scalar_tensor_tensor(o, i, ISQ, m, ALU.mult, ALU.add),
                             waits=later(tok["sc"][w], t_m0, t_mm, tok["ex"].get(w - 4)), inc=True)
        tok["mx"][w] = S.add("vector",
                             lambda e, o=sb_ngmx[u4][:, :], i=sb_ssb[u4][:, :]:
                             e.reduce_max(o, i, axis=AX.X, negate=True),
                             waits=later(tok["ex"].get(w - 4)), inc=True)

    def st_E(w):
        u4, u8 = w % 4, w % 8
        tok["ex"][w] = S.add("scalar",
                             lambda e, o=sb_psb[u8][:, :], i=sb_ssb[u4][:, :], b=sb_ngmx[u4][:, :]:
                             e.activation(o, i, AF.Exp, bias=b, scale=1.0),
                             waits=later(tok["mx"][w], tok["nr"].get(w - 8)), inc=True)

    def st_SM(w):
        u4, u8 = w % 4, w % 8
        tok["sm"][w] = S.add("vector",
                             lambda e, o=sb_sume[u4][:, :], i=sb_psb[u8][:, :]:
                             e.reduce_sum(o, i, axis=AX.X),
                             waits=later(tok["ex"][w], tok["ln"].get(w - 4)), inc=True)

    def st_L(w):
        u4, u8 = w % 4, w % 8
        tok["ln"][w] = S.add("scalar",
                             lambda e, o=sb_rsum[u8][:, :], i=sb_sume[u4][:, :], b=sb_zero[:, :]:
                             e.activation(o, i, AF.Ln, bias=b, scale=1.0),
                             waits=later(tok["sm"][w], t_zero, tok["nr"].get(w - 8)), inc=True)
        S.add("scalar", lambda e, io=sb_rsum[u8][:, :], b=sb_zero[:, :]:
              e.activation(io, io, AF.Exp, bias=b, scale=-1.0), inc=True)
        tok["ln"][w] = ("scalar", S.cnt["scalar"])

    def st_RC(w):
        u8 = w % 8
        tok["rc"][w] = S.add("vector",
                             lambda e, i=sb_rsum[u8][:, :]: e.tensor_copy(sb_trash[:, :], i),
                             waits=later(tok["ln"][w]), inc=True)

    def st_N(w):
        u4, u8 = w % 4, w % 8
        tok["nr"][w] = S.add("scalar",
                             lambda e, o=sb_abf[u4][:, :], i=sb_psb[u8][:, :], s=sb_rsum[u8][:, :]:
                             e.activation(o, i, AF.Copy, bias=0.0, scale=s),
                             waits=later(tok["rc"][w], tok["at"].get(w - 4)), inc=True)

    def st_T(w):
        u4 = w % 4
        ta1 = S.add("tensor", lambda e, o=p_at[w % 2][:, :], i=sb_abf[u4][:, 11:139]:
                    e.transpose(o, i, sb_idbf[:, :]),
                    waits=later(tok["nr"][w], tok["av"].get(w - 2)), inc=True)
        ea1 = S.add("vector", lambda e, d=sb_at1[u4][:, :], s=p_at[w % 2][:, :]:
                    e.tensor_copy(d, s), waits=later(ta1), inc=True)
        ta2 = S.add("tensor", lambda e, o=p_at[w % 2][0:11, :], i=sb_abf[u4][:, 0:11]:
                    e.transpose(o, i, sb_idbf[:, :]), waits=later(ea1), inc=True)
        ea2 = S.add("vector", lambda e, d=sb_at2[u4][:, :], s=p_at[w % 2][0:11, :]:
                    e.tensor_copy(d, s), waits=later(ta2), inc=True)
        tok["at"][w] = later(ea1, ea2)

    def st_V(w):
        j, qi = w // 2, w % 2
        tt, kv = iters[j]
        qh = 2 * kv + qi
        u4, ip = w % 4, j % 8
        S.add("tensor",
              lambda e, o=p_av[w % 2][:, :], l=sb_vt1[ip][:, :], r=sb_at1[u4][:, :]:
              e.matmul(o, l, r, start=True, stop=False),
              waits=later(tok_scr[j], tok["at"][w], t_pav[w % 2]))
        tav = S.add("tensor",
                    lambda e, o=p_av[w % 2][:, :], l=sb_vt2[ip][:, :], r=sb_at2[u4][:, :]:
                    e.matmul(o, l, r, start=False, stop=True), inc=True)
        eav = S.add("vector",
                    lambda e, d=sb_ao[qh][:, tt * 128:(tt + 1) * 128], s=p_av[w % 2][:, :]:
                    e.tensor_copy(d, s), waits=later(tav), inc=True)
        t_pav[w % 2] = eav
        t_attn_out.append(eav)
        tok["av"][w] = tav
        if qi == 1:
            tok_scr_free[j] = tav

    STAGES = [st_Q, st_M, st_E, st_SM, st_L, st_RC, st_N, st_T, st_V]
    NSTG = len(STAGES)
    stage_A(0)
    stage_A(1)
    for w in range(W + NSTG - 1):
        if w % 2 == 0 and w // 2 + 2 < NIT + 2:
            stage_A(w // 2 + 2)
        for si, fn in enumerate(STAGES):
            ww = w - si
            if 0 <= ww < W:
                fn(ww)
    t_attn_all = later(*t_attn_out)

    pst.close()
    pst = ExitStack()
    p_wo = [pst.enter_context(nc.psum_tensor(f"p_wo{i}", [128, 512], F32)) for i in range(2)]
    p_t32 = [pst.enter_context(nc.psum_tensor(f"p_t32{i}", [128, 128], F32)) for i in range(2)]
    p_tb3 = [pst.enter_context(nc.psum_tensor(f"p_tb3{i}", [128, 128], BF16)) for i in range(2)]
    p_cs = [pst.enter_context(nc.psum_tensor(f"p_cs{i}", [1, 512], F32)) for i in range(2)]
    t_pwo = [None, None]
    t_pt32 = [None, None]
    t_ptb3 = [None, None]
    t_wo = [None, None]
    t_ofm = []
    bank = 0

    def wo_load(i, oc, free_tok):
        t_wo[i] = S.add("gpsimd",
                        lambda e, d=sb_wb[i][:, :], s=d_wo[oc, :, :]: e.dma_start(out=d, in_=s),
                        waits=later(free_tok), inc=True, inc_by=16)

    wo_load(0, 0, t_wb_free_qkv[0])
    wo_load(1, 1, t_wb_free_qkv[1])
    for oc in range(16):
        wb = oc % 2
        t_w = t_wo[wb]
        t_grp = None
        for th in range(2):
            ps = p_wo[bank % 2]
            for kc in range(NKC):
                w = later(t_w, t_attn_all, t_pwo[bank % 2]) if kc == 0 else ()
                S.add("tensor",
                      lambda e, o=ps[:, :], l=sb_wb[wb][:, kc * 128:(kc + 1) * 128],
                      r=sb_ao[kc][:, th * 512:(th + 1) * 512], k=kc:
                      e.matmul(o, l, r, start=(k == 0), stop=(k == NKC - 1)),
                      waits=w, inc=(kc == NKC - 1))
            t_grp = ("tensor", S.cnt["tensor"])
            tev = S.add("vector",
                        lambda e, d=sb_ofm[oc][:, th * 512:(th + 1) * 512], s=ps[:, :]:
                        e.tensor_copy(d, s), waits=later(t_grp), inc=True)
            t_pwo[bank % 2] = tev
            t_ofm.append(tev)
            bank += 1
        if oc + 2 < 16:
            wo_load(wb, oc + 2, t_grp)
    t_ofm_all = later(*t_ofm)

    t_otm_free = [None, None]
    t_park = []
    t_ln3T_done = []
    sq_freeB = {"t": None}
    sq4 = [(sb_sq, sq_free), (sb_sqB, sq_freeB)]
    for tt in range(NT):
        par = tt % 2
        sqt4, sqf4 = sq4[par]
        c = slice(par, par + 1)
        sb_otm = sb_otm2[par]
        evs = []
        for fc in range(NKC):
            pp = (tt * NKC + fc) % 2
            tmm = S.add("tensor",
                        lambda e, o=p_t32[pp][:, :], i=sb_ofm[fc][:, tt * 128:(tt + 1) * 128]:
                        e.transpose(o, i, sb_id32[:, :]),
                        waits=later(t_ofm_all, t_pt32[pp], t_otm_free[par], t_id32), inc=True)
            tev = S.add("vector",
                        lambda e, d=sb_otm[:, fc * 128:(fc + 1) * 128], s=p_t32[pp][:, :]:
                        e.tensor_copy(d, s), waits=later(tmm), inc=True)
            t_pt32[pp] = tev
            evs.append(tev)
        t_otm = later(*evs)
        a_nsum, a_nm, a_v2k = sb_nsum[par][:, :], sb_nm[par][:, :], sb_v2k[par][:, :]
        a_lnv, a_rstd, a_nmr = sb_lnv[par][:, :], sb_rstd[par][:, :], sb_nmr[par][:, :]
        t0 = S.add("vector", lambda e, o=a_nsum, i=sb_otm[:, :]: e.reduce_sum(o, i, axis=AX.X, negate=True),
                   waits=later(t_otm), inc=True)
        tm = S.add("scalar", lambda e, o=a_nm, i=a_nsum: e.mul(o, i, 1.0 / H),
                   waits=later(t0), inc=True)
        tb = S.add("vector", lambda e, o=a_nsum, i=a_nm: e.tensor_copy(o, i),
                   waits=later(tm), inc=True)
        tsq = S.add("scalar", lambda e, b=a_nm, i=sb_otm[:, :], sq=sqt4[:, :]:
                    e.activation(sq, i, AF.Square, bias=b, scale=1.0),
                    waits=later(sqf4["t"], tb), inc=True)
        tvv = S.add("vector", lambda e, o=a_v2k, i=sqt4[:, :]: e.reduce_sum(o, i, axis=AX.X),
                    waits=later(tsq), inc=True)
        sqf4["t"] = tvv
        S.add("scalar", lambda e, o=a_lnv, i=a_v2k, b=sb_eps[:, :]:
              e.activation(o, i, AF.Ln, bias=b, scale=1.0 / H),
              waits=later(t_eps, tvv), inc=True)
        t4 = S.add("scalar", lambda e, o=a_rstd, i=a_lnv, b=sb_zero[:, :]:
                   e.activation(o, i, AF.Exp, bias=b, scale=-0.5),
                   waits=later(t_zero), inc=True)
        t5 = S.add("vector", lambda e, o=a_nmr, a=a_nm, b=a_rstd: e.tensor_mul(o, a, b),
                   waits=later(t4), inc=True)
        t6 = S.add("scalar", lambda e, b=a_nmr, s=a_rstd, io=sb_otm[:, :]:
                   e.activation(io, io, AF.Identity, bias=b, scale=s),
                   waits=later(t5), inc=True)
        t7 = S.add("vector", lambda e, x=sb_x[tt][:, :], o=sb_otm[:, :]: e.tensor_add(x, x, o),
                   waits=later(t6), inc=True)
        t_otm_free[par] = t7
        tp = S.add("sync", lambda e, d=d_xp[tt * 128:(tt + 1) * 128, :], s=sb_x[tt][:, :]:
                   e.dma_start(out=d, in_=s), waits=later(t7), inc=True, inc_by=16)
        t_park.append(tp)
        t_ln = ln_tile(sb_x[tt][:, :], 128, par, later(t7, t_lnbf_free[par], t_attn_all),
                       sb_lnbf[par][:, :], sqt=sqt4, sqf=sqf4)
        for fc in range(NKC):
            pp = (tt * NKC + fc) % 2
            tmm = S.add("tensor",
                        lambda e, o=p_tb3[pp][:, :], i=sb_lnbf[par][:, fc * 128:(fc + 1) * 128]:
                        e.transpose(o, i, sb_idbf[:, :]),
                        waits=later(t_ln, t_ptb3[pp]), inc=True)
            tev = S.add("vector",
                        lambda e, d=sb_ln3T[fc][:, tt * 128:(tt + 1) * 128], s=p_tb3[pp][:, :]:
                        e.tensor_copy(d, s), waits=later(tmm), inc=True)
            t_ptb3[pp] = tev
            t_ln3T_done.append(tev)
            if fc == NKC - 1:
                t_lnbf_free[par] = tmm
    t_ln3T_all = later(*t_ln3T_done)
    t_park_all = later(*t_park)

    t_csum = []
    for th in range(2):
        for kc in range(NKC):
            w = later(t_ln3T_all, t_ones) if kc == 0 else ()
            S.add("tensor",
                  lambda e, o=p_cs[th][0:1, :], l=sb_ones[:, 0:1],
                  r=sb_ln3T[kc][:, th * 512:(th + 1) * 512], k=kc:
                  e.matmul(o, l, r, start=(k == 0), stop=(k == NKC - 1)),
                  waits=w, inc=(kc == NKC - 1))
        t_grp = ("tensor", S.cnt["tensor"])
        tev = S.add("vector",
                    lambda e, d=sb_csum[0:1, th * 512:(th + 1) * 512], s=p_cs[th][0:1, :]:
                    e.tensor_copy(d, s), waits=later(t_grp), inc=True)
        t_csum.append(tev)
    t_csum_all = later(*t_csum)

    pst.close()
    pst = ExitStack()
    p_u = [pst.enter_context(nc.psum_tensor(f"p_u{i}", [128, 512], F32)) for i in range(4)]
    p_g = [pst.enter_context(nc.psum_tensor(f"p_g{i}", [128, 512], F32)) for i in range(4)]
    t_pu = [None] * 4
    t_pg = [None] * 4
    t_uact_free = None
    t_sact_free = None
    t_h = [None] * NIC
    t_wgl = {}
    t_wul = {}

    def gu_load(tab, bufs, dten, ic, i, free_tok):
        tab[ic] = S.add("gpsimd",
                        lambda e, d=bufs[i][:, :], s=dten[ic, :, :]: e.dma_start(out=d, in_=s),
                        waits=later(free_tok), inc=True, inc_by=16)

    gu_load(t_wgl, sb_wgb, d_wg, 0, 0, later(t_ofm_all))
    gu_load(t_wgl, sb_wgb, d_wg, 1, 1, later(t_ofm_all))
    gu_load(t_wul, sb_wub, d_wu, 0, 0, later(t_attn_all, t_ln3T_all))
    gu_load(t_wul, sb_wub, d_wu, 1, 1, later(t_attn_all, t_ln3T_all))
    for ic in range(NIC):
        wb = ic % 2
        bk = ic % 2
        for th in range(2):
            ps = p_u[2 * bk + th]
            for kc in range(NKC):
                w = later(t_wul[ic], t_ln3T_all, t_csum_all, t_n128,
                          t_pu[2 * bk + th], t_park_all) if kc == 0 else ()
                S.add("tensor",
                      lambda e, o=ps[:, :], l=sb_wub[wb][:, kc * 128:(kc + 1) * 128],
                      r=sb_ln3T[kc][:, th * 512:(th + 1) * 512], k=kc:
                      e.matmul(o, l, r, start=(k == 0), stop=False), waits=w)
            S.add("tensor",
                  lambda e, o=ps[:, :], l=sb_n128[0:1, :], r=sb_csum[0:1, th * 512:(th + 1) * 512]:
                  e.matmul(o, l, r, start=False, stop=True), inc=True)
        t_ugrp = ("tensor", S.cnt["tensor"])
        if ic + 2 < NIC:
            gu_load(t_wul, sb_wub, d_wu, ic + 2, wb, t_ugrp)
        ues = []
        for th in range(2):
            tev = S.add("scalar",
                        lambda e, d=sb_uact[:, th * 512:(th + 1) * 512], s=p_u[2 * bk + th][:, :],
                        sc=sb_us[:, ic:ic + 1]:
                        e.activation(d, s, AF.Copy, bias=0.0, scale=sc),
                        waits=later(t_ugrp, t_uact_free, t_usl), inc=True)
            t_pu[2 * bk + th] = tev
            ues.append(tev)
        for th in range(2):
            ps = p_g[2 * bk + th]
            for kc in range(NKC):
                w = later(t_wgl[ic], t_pg[2 * bk + th]) if kc == 0 else ()
                S.add("tensor",
                      lambda e, o=ps[:, :], l=sb_wgb[wb][:, kc * 128:(kc + 1) * 128],
                      r=sb_ln3T[kc][:, th * 512:(th + 1) * 512], k=kc:
                      e.matmul(o, l, r, start=(k == 0), stop=False), waits=w)
            S.add("tensor",
                  lambda e, o=ps[:, :], l=sb_n128[0:1, :], r=sb_csum[0:1, th * 512:(th + 1) * 512]:
                  e.matmul(o, l, r, start=False, stop=True), inc=True)
        t_ggrp = ("tensor", S.cnt["tensor"])
        if ic + 2 < NIC:
            gu_load(t_wgl, sb_wgb, d_wg, ic + 2, wb, t_ggrp)
        for th in range(2):
            tsv = S.add("scalar",
                        lambda e, d=sb_sact[:, th * 512:(th + 1) * 512], s=p_g[2 * bk + th][:, :],
                        sc=sb_gs[:, ic:ic + 1]:
                        e.activation(d, s, AF.Silu, bias=sb_zero[:, :], scale=sc),
                        waits=later(t_ggrp, t_sact_free, t_gsl, t_zero), inc=True)
            t_pg[2 * bk + th] = tsv
            thm = S.add("vector",
                        lambda e, d=sb_h[ic][:, th * 512:(th + 1) * 512],
                        a=sb_sact[:, th * 512:(th + 1) * 512], b=sb_uact[:, th * 512:(th + 1) * 512]:
                        e.tensor_mul(d, a, b),
                        waits=later(tsv, ues[th], t_park_all), inc=True)
            if th == 1:
                t_h[ic] = thm
                t_uact_free = thm
                t_sact_free = thm
    t_h_all = later(*t_h)
    t_gu_tensor_done = ("tensor", S.cnt["tensor"])

    pst.close()
    pst = ExitStack()
    p_hs = [pst.enter_context(nc.psum_tensor(f"p_hs{i}", [1, 512], F32)) for i in range(2)]
    p_d = [pst.enter_context(nc.psum_tensor(f"p_d{i}", [128, 256], F32)) for i in range(2)]
    p_t32b = [pst.enter_context(nc.psum_tensor(f"p_t32b{i}", [128, 128], F32)) for i in range(2)]

    t_hsum = []
    for th in range(2):
        for ic in range(NIC):
            S.add("tensor",
                  lambda e, o=p_hs[th][0:1, :], l=sb_ones[:, 0:1],
                  r=sb_h[ic][:, th * 512:(th + 1) * 512], k=ic:
                  e.matmul(o, l, r, start=(k == 0), stop=(k == NIC - 1)),
                  waits=(later(t_h_all) if ic == 0 else ()), inc=(ic == NIC - 1))
        t_grp = ("tensor", S.cnt["tensor"])
        tev = S.add("vector",
                    lambda e, d=sb_hsum[0:1, th * 512:(th + 1) * 512], s=p_hs[th][0:1, :]:
                    e.tensor_copy(d, s), waits=later(t_grp), inc=True)
        t_hsum.append(tev)
    t_hsum_all = later(*t_hsum)
    t_pd = [None, None]
    t_pt32b = [None, None]
    t_dq_free = [None] * 16
    t_xpr_free = [None, None]
    t_dtm_free = [None, None]
    t_out = []
    wd_seq = [(tq, hf, half) for tq in range(4) for hf in range(16) for half in range(2)]
    t_wdl = {}

    def wd_load(seq_i, free_tok):
        tq_, hf_, half_ = wd_seq[seq_i]
        t_wdl[seq_i] = S.add("gpsimd",
                             lambda e, d=sb_wdb[seq_i % 2][:, :],
                             s=d_wd[hf_, :, half_ * 4096:(half_ + 1) * 4096]:
                             e.dma_start(out=d, in_=s),
                             waits=later(free_tok), inc=True, inc_by=16)

    wd_load(0, later(t_gu_tensor_done))
    wd_load(1, later(t_gu_tensor_done))
    for tq in range(4):
        for hf in range(16):
            seq_i = tq * 32 + hf * 2
            pb = hf % 2
            ps = p_d[pb]
            for half in range(2):
                t_w = t_wdl[seq_i + half]
                for j in range(32):
                    ic = half * 32 + j
                    if half == 0 and j == 0:
                        w = later(t_w, t_h_all, t_hsum_all, t_pd[pb])
                    elif j == 0:
                        w = later(t_w)
                    else:
                        w = ()
                    S.add("tensor",
                          lambda e, o=ps[:, :], l=sb_wdb[(seq_i + half) % 2][:, j * 128:(j + 1) * 128],
                          r=sb_h[ic][:, tq * 256:(tq + 1) * 256], st=(half == 0 and j == 0):
                          e.matmul(o, l, r, start=st, stop=False),
                          waits=w, inc=(j == 31))
                t_half = ("tensor", S.cnt["tensor"])
                if seq_i + half + 2 < len(wd_seq):
                    wd_load(seq_i + half + 2, t_half)
            S.add("tensor",
                  lambda e, o=ps[:, :], l=sb_n128[0:1, :], r=sb_hsum[0:1, tq * 256:(tq + 1) * 256]:
                  e.matmul(o, l, r, start=False, stop=True), inc=True)
            t_grp = ("tensor", S.cnt["tensor"])
            tev = S.add("scalar",
                        lambda e, d=sb_dq[hf][:, :], s=ps[:, :], sc=sb_ds[:, hf:hf + 1]:
                        e.activation(d, s, AF.Copy, bias=0.0, scale=sc),
                        waits=later(t_grp, t_dq_free[hf], t_dsl), inc=True)
            t_pd[pb] = tev
            t_dq_free[hf] = None
        t_dq_all = ("scalar", S.cnt["scalar"])
        for sub in range(2):
            tt = tq * 2 + sub
            par = tt % 2
            c = slice(par, par + 1)
            sb_dtm = sb_dtm2[par]
            txp = S.add("sync",
                        lambda e, d=sb_xpr[par][:, :], s=d_xp[tt * 128:(tt + 1) * 128, :]:
                        e.dma_start(out=d, in_=s),
                        waits=later(t_park_all, t_xpr_free[par], t_h_all), inc=True, inc_by=16)
            evs = []
            for fc in range(NKC):
                pp = (tt * NKC + fc) % 2
                tmm = S.add("tensor",
                            lambda e, o=p_t32b[pp][:, :], i=sb_dq[fc][:, sub * 128:(sub + 1) * 128]:
                            e.transpose(o, i, sb_id32[:, :]),
                            waits=later(t_dq_all, t_pt32b[pp], t_dtm_free[par]), inc=True)
                tev = S.add("vector",
                            lambda e, d=sb_dtm[:, fc * 128:(fc + 1) * 128], s=p_t32b[pp][:, :]:
                            e.tensor_copy(d, s), waits=later(tmm), inc=True)
                t_pt32b[pp] = tev
                evs.append(tev)
                if sub == 1:
                    t_dq_free[fc] = tmm
            t_dtm = later(*evs)
            a_nsum, a_nm, a_v2k = sb_nsum[par][:, :], sb_nm[par][:, :], sb_v2k[par][:, :]
            a_lnv, a_rstd, a_nmr = sb_lnv[par][:, :], sb_rstd[par][:, :], sb_nmr[par][:, :]
            t0 = S.add("vector", lambda e, o=a_nsum, i=sb_dtm[:, :]: e.reduce_sum(o, i, axis=AX.X, negate=True),
                  waits=later(t_dtm), inc=True)
            tm = S.add("scalar", lambda e, o=a_nm, i=a_nsum: e.mul(o, i, 1.0 / H),
                  waits=later(t0), inc=True)
            tb = S.add("vector", lambda e, o=a_nsum, i=a_nm: e.tensor_copy(o, i),
                       waits=later(tm), inc=True)
            tsq = S.add("scalar", lambda e, b=a_nm, i=sb_dtm[:, :]:
                        e.activation(sb_sq[:, :], i, AF.Square, bias=b, scale=1.0),
                        waits=later(sq_free["t"], tb), inc=True)
            tvv = S.add("vector", lambda e, o=a_v2k: e.reduce_sum(o, sb_sq[:, :], axis=AX.X),
                        waits=later(tsq), inc=True)
            sq_free["t"] = tvv
            S.add("scalar", lambda e, o=a_lnv, i=a_v2k, b=sb_eps[:, :]:
                  e.activation(o, i, AF.Ln, bias=b, scale=1.0 / H),
                  waits=later(t_eps, tvv), inc=True)
            t4 = S.add("scalar", lambda e, o=a_rstd, i=a_lnv, b=sb_zero[:, :]:
                       e.activation(o, i, AF.Exp, bias=b, scale=-0.5),
                       waits=later(t_zero), inc=True)
            t5 = S.add("vector", lambda e, o=a_nmr, a=a_nm, b=a_rstd: e.tensor_mul(o, a, b),
                       waits=later(t4), inc=True)
            t6 = S.add("scalar", lambda e, b=a_nmr, s=a_rstd, io=sb_dtm[:, :]:
                       e.activation(io, io, AF.Identity, bias=b, scale=s),
                       waits=later(t5), inc=True)
            t7 = S.add("vector", lambda e, xp=sb_xpr[par][:, :], o=sb_dtm[:, :]:
                       e.tensor_add(xp, xp, o), waits=later(t6, txp), inc=True)
            t_dtm_free[par] = t7
            to = S.add("sync", lambda e, d=d_out[tt * 128:(tt + 1) * 128, :], s=sb_xpr[par][:, :]:
                       e.dma_start(out=d, in_=s), waits=later(t7), inc=True, inc_by=16)
            t_xpr_free[par] = to
            t_out.append(to)

    final = S.cnt["sync"]
    S.add("sync", lambda e: e.nop(), waits=[("sync", final)])

    sem_stack = ExitStack()
    sems = {e: sem_stack.enter_context(nc.semaphore(f"sem_{e}")) for e in ENGINES}
    S.emit(nc, sems)
    sem_stack.close()
    pst.close()
    return nc


def _host_layout(inputs):
    x = np.ascontiguousarray(np.asarray(inputs["L0_attn_x"]).reshape(S_TOT, H))
    Wq = np.asarray(inputs["L0_attn_Wq"])
    Wk = np.asarray(inputs["L0_attn_Wk"])
    Wv = np.asarray(inputs["L0_attn_Wv"])
    Wo = np.asarray(inputs["L0_attn_Wo"])
    gw = np.asarray(inputs["L0_ffn_gate_w"])
    uw = np.asarray(inputs["L0_ffn_up_w"])
    dw = np.asarray(inputs["L0_ffn_down_w"])
    gsc = np.asarray(inputs["L0_ffn_gate_scale"])
    usc = np.asarray(inputs["L0_ffn_up_scale"])
    dsc = np.asarray(inputs["L0_ffn_down_scale"])

    def wtile(W, noc):
        return np.ascontiguousarray(
            W.reshape(noc, 128, NKC, 128).transpose(0, 3, 2, 1).reshape(noc, 128, 2048))

    wq, wk, wv, wo = wtile(Wq, 16), wtile(Wk, 8), wtile(Wv, 8), wtile(Wo, 16)

    def u8_low(a):
        return np.ascontiguousarray(a.view(np.uint8).reshape(a.shape[0], a.shape[1], 4)[:, :, 0])

    g8, u8, dn8 = u8_low(gw), u8_low(uw), u8_low(dw)
    wg8 = np.ascontiguousarray(g8.reshape(NIC, 128, NKC, 128).transpose(0, 3, 2, 1).reshape(NIC, 128, 2048))
    wu8 = np.ascontiguousarray(u8.reshape(NIC, 128, NKC, 128).transpose(0, 3, 2, 1).reshape(NIC, 128, 2048))
    wd8 = np.ascontiguousarray(dn8.reshape(16, 128, NIC, 128).transpose(0, 3, 2, 1).reshape(16, 128, 8192))

    gs = np.ascontiguousarray(gsc.reshape(NIC, 128).T)
    us = np.ascontiguousarray(usc.reshape(NIC, 128).T)
    ds = np.ascontiguousarray(dsc.reshape(16, 128).T)

    r = np.arange(128)[:, None]
    cc = np.arange(NKEY)[None, :]
    maskm = np.full((128, NKEY), NEG, dtype=np.float32)
    maskm[:, 0:4] = 0.0
    band = (cc >= r + 4) & (cc <= r + 11) & (cc >= 4)
    maskm[band] = 0.0
    mask0 = np.full((128, NKEY), NEG, dtype=np.float32)
    j = cc - 11
    ok0 = (cc >= 11) & (j <= r) & ((j < 4) | (r - j < 8))
    mask0[ok0] = 0.0

    in_maps = []
    for core in range(NCORE):
        s0 = core * T
        xh = np.empty((11, H), dtype=np.float32)
        xh[0:7] = x[0:7] if core == 0 else x[s0 - 7:s0]
        xh[7:11] = x[0:4]
        in_maps.append({
            "x": np.ascontiguousarray(x[s0:s0 + T]),
            "xh": xh,
            "wq": wq, "wk": wk, "wv": wv, "wo": wo,
            "wg8": wg8, "wu8": wu8, "wd8": wd8,
            "gs": gs, "us": us, "ds": ds,
            "mask0": np.ascontiguousarray(mask0 if core == 0 else maskm),
            "maskm": maskm,
        })
    return in_maps


def kernel(**inputs):
    in_maps = _host_layout(inputs)
    nc = build_nc()
    res = run_bass_kernel_spmd(nc, in_maps, core_ids=list(range(NCORE)))
    out = np.concatenate([r["out"] for r in res.results], axis=0)
    return out.reshape(1, S_TOT, H).astype(np.float32, copy=False)


# version 16 (final)
# speedup vs baseline: 1.0033x; 1.0033x over previous
import numpy as np
import ml_dtypes
from contextlib import ExitStack
import concourse.bass as bass
import concourse.mybir as mybir
from concourse.bass_utils import run_bass_kernel_spmd

F32 = mybir.dt.float32
BF16 = mybir.dt.bfloat16
U8 = mybir.dt.uint8
AF = mybir.ActivationFunctionType
AX = mybir.AxisListType
ALU = mybir.AluOpType

H = 2048
S_TOT = 8192
NCORE = 8
T = 1024
NT = 8
NKC = 16
NQH = 16
NKV = 8
FFI = 8192
NIC = 64
EPS = 1e-5
ISQ = float(1.0 / np.sqrt(128.0))
NEG = -1.0e30
NKEY = 139

BASE = 16512
SB_TOP = 229376

ENGINES = ("sync", "gpsimd", "tensor", "vector", "scalar")
COMPUTE = ("tensor", "vector", "scalar")


class Sched:
    DEDUP = True

    def __init__(self):
        self.ops = {e: [] for e in ENGINES}
        self.cnt = {e: 0 for e in ENGINES}
        self.n = 0

    def add(self, eng, fn, waits=(), inc=False, inc_by=1):
        tok = None
        if inc:
            self.cnt[eng] += inc_by
            tok = (eng, self.cnt[eng])
        ws = [(s, v) for (s, v) in waits if v > 0]
        self.ops[eng].append((ws, fn, self.cnt[eng] if inc else None, inc_by))
        self.n += 1
        return tok

    def emit(self, nc, sems):
        import os as _os
        simmode = _os.environ.get("BASS_SIM_FP64") == "1"
        with nc.Block() as block:
            for eng in ENGINES:
                ops = self.ops[eng]
                if not ops:
                    continue

                def body(e, ops=ops, eng=eng):
                    hi = {k: 0 for k in ENGINES}
                    own = 0
                    for ws, fn, inc_to, inc_by in ops:
                        if simmode and own > hi[eng]:
                            e.wait_ge(sems[eng], own)
                            hi[eng] = own
                        for s, v in ws:
                            if s == eng and not simmode:
                                continue
                            if v > hi[s] or not Sched.DEDUP:
                                e.wait_ge(sems[s], v)
                                hi[s] = max(hi[s], v)
                        ins = fn(e)
                        if inc_to is not None:
                            ins.then_inc(sems[eng], inc_by)
                            own = inc_to

                getattr(block, eng)(body)


def later(*toks):
    best = {}
    for t in toks:
        if t is None:
            continue
        if isinstance(t, list):
            for s, v in t:
                if v > best.get(s, -1):
                    best[s] = v
            continue
        s, v = t
        if v > best.get(s, -1):
            best[s] = v
    return [(s, v) for s, v in best.items()]


def build_nc():
    nc = bass.Bass("TRN2", target_bir_lowering=False)

    d_x = nc.dram_tensor("x", [T, H], F32, kind="ExternalInput")
    d_xh = nc.dram_tensor("xh", [11, H], F32, kind="ExternalInput")
    d_wq = nc.dram_tensor("wq", [16, 128, 2048], F32, kind="ExternalInput")
    d_wk = nc.dram_tensor("wk", [8, 128, 2048], F32, kind="ExternalInput")
    d_wv = nc.dram_tensor("wv", [8, 128, 2048], F32, kind="ExternalInput")
    d_wo = nc.dram_tensor("wo", [16, 128, 2048], F32, kind="ExternalInput")
    d_wg = nc.dram_tensor("wg8", [64, 128, 2048], U8, kind="ExternalInput")
    d_wu = nc.dram_tensor("wu8", [64, 128, 2048], U8, kind="ExternalInput")
    d_wd = nc.dram_tensor("wd8", [16, 128, 8192], U8, kind="ExternalInput")
    d_gs = nc.dram_tensor("gs", [128, 64], F32, kind="ExternalInput")
    d_us = nc.dram_tensor("us", [128, 64], F32, kind="ExternalInput")
    d_ds = nc.dram_tensor("ds", [128, 16], F32, kind="ExternalInput")
    d_m0 = nc.dram_tensor("mask0", [128, NKEY], F32, kind="ExternalInput")
    d_mm = nc.dram_tensor("maskm", [128, NKEY], F32, kind="ExternalInput")
    d_out = nc.dram_tensor("out", [T, H], F32, kind="ExternalOutput")
    d_xp = nc.dram_tensor("xp", [T, H], F32, kind="Internal")

    ident_np = np.eye(128, dtype=np.float32)
    d_id32 = nc.inline_tensor(ident_np, "id32")
    d_idbf = nc.inline_tensor(ident_np.astype(ml_dtypes.bfloat16), "idbf")

    def at(name, shape, dtype, off):
        end = off + int(np.prod(shape[1:])) * mybir.dt.size(dtype)
        assert BASE <= off and end <= SB_TOP, (name, off, end)
        return nc.alloc_sbuf_tensor_at(name, shape, dtype, offset=off)

    P0 = BASE
    sb_m0 = at("sb_m0", [128, NKEY], F32, P0 + 0)
    sb_mm = at("sb_mm", [128, NKEY], F32, P0 + 576)
    sb_idbf = at("sb_idbf", [128, 128], BF16, P0 + 1152)
    sb_id32 = at("sb_id32", [128, 128], F32, P0 + 1408)
    sb_gs = at("sb_gs", [128, 64], F32, P0 + 1920)
    sb_us = at("sb_us", [128, 64], F32, P0 + 2176)
    sb_ds = at("sb_ds", [128, 16], F32, P0 + 2432)
    sb_ones = at("sb_ones", [128, 2], BF16, P0 + 2496)
    sb_n128 = at("sb_n128", [1, 128], BF16, P0 + 2560)
    def statpair(name, off):
        return [at(name + str(i), [128, 1], F32, off + 32 * i) for i in range(2)]

    sb_nsum = statpair("sb_nsum", P0 + 2816)
    sb_nm = statpair("sb_nm", P0 + 2880)
    sb_v2k = statpair("sb_v2k", P0 + 2944)
    sb_lnv = statpair("sb_lnv", P0 + 3008)
    sb_rstd = statpair("sb_rstd", P0 + 3072)
    sb_nmr = statpair("sb_nmr", P0 + 3136)

    sb_zero = at("sb_zero", [128, 1], F32, P0 + 3520)
    sb_eps = at("sb_eps", [128, 1], F32, P0 + 3552)
    sb_csum = at("sb_csum", [1, 1024], BF16, P0 + 3584)
    sb_hsum = at("sb_hsum", [1, 1024], BF16, P0 + 5632)
    sb_sq = at("sb_sq", [128, 2048], BF16, P0 + 7680)
    sb_xh = at("sb_xh", [11, H], F32, P0 + 11776)
    sb_sqB = at("sb_sqB", [128, 2048], BF16, P0 + 11776)

    A0 = P0 + 19968
    sb_x = [at(f"sb_x{t}", [128, H], F32, A0 + t * 8192) for t in range(NT)]

    B0 = A0 + 65536
    sb_ln1T = [at(f"sb_ln1T{c}", [128, 1040], BF16, B0 + c * 2080) for c in range(NKC)]
    QF0 = B0 + 33280
    sb_q = [at(f"sb_q{h}", [128, 1024], BF16, QF0 + h * 2048) for h in range(NQH)]
    KF0 = QF0 + 32768
    sb_k = [at(f"sb_k{h}", [128, 1040], BF16, KF0 + h * 2080) for h in range(NKV)]
    VF0 = KF0 + 16640
    sb_v = [at(f"sb_v{h}", [128, 1040], BF16, VF0 + h * 2080) for h in range(NKV)]

    C0 = VF0 + 16640
    SL0, SL1, SL2 = C0, C0 + 8192, C0 + 16384
    sb_wb = [at(f"sb_wb{i}", [128, 2048], BF16, SL0 + i * 4096) for i in range(2)]
    sb_wgb = [at(f"sb_wgb{i}", [128, 2048], BF16, SL0 + i * 4096) for i in range(2)]
    sb_dtm2 = [at("sb_dtm", [128, 2048], F32, SL0),
               at("sb_dtmB", [128, 2048], F32, P0 + 11776)]
    sb_lnbf = [at(f"sb_lnbf{i}", [128, 2048], BF16, SL1 + i * 4096) for i in range(2)]
    AT0 = SL1
    sb_ssb = [at(f"sb_ssb{i}", [128, NKEY], F32, AT0 + i * 576) for i in range(4)]
    sb_psb = [at(f"sb_psb{i}", [128, NKEY], F32, AT0 + 2304 + i * 576) for i in range(8)]
    sb_abf = [at(f"sb_abf{i}", [128, NKEY], BF16, AT0 + 6912 + i * 288) for i in range(4)]
    sb_at1 = [at(f"sb_at1{i}", [128, 128], BF16, AT0 + 8064 + i * 256) for i in range(4)]
    sb_at2 = [at(f"sb_at2{i}", [11, 128], BF16, AT0 + 9088 + i * 256) for i in range(4)]
    sb_ngmx = [at(f"sb_ngmx{i}", [128, 1], F32, AT0 + 10112 + i * 32) for i in range(4)]
    sb_sume = [at(f"sb_sume{i}", [128, 1], F32, AT0 + 10240 + i * 32) for i in range(4)]
    sb_rsum = [at(f"sb_rsum{i}", [128, 1], F32, AT0 + 10368 + i * 32) for i in range(8)]
    sb_kscr = [at(f"sb_kscr{i}", [128, NKEY], BF16, AT0 + 10624 + i * 288) for i in range(8)]
    sb_vscr = [at(f"sb_vscr{i}", [128, NKEY], BF16, AT0 + 12928 + i * 288) for i in range(8)]
    sb_vt1 = [at(f"sb_vt1{i}", [128, 128], BF16, AT0 + 15232 + i * 256) for i in range(8)]
    sb_vt2 = [at(f"sb_vt2{i}", [11, 128], BF16, AT0 + 17280 + i * 256) for i in range(8)]
    sb_trash = at("sb_trash", [128, 1], F32, AT0 + 19328)
    sb_wub = [at(f"sb_wub{i}", [128, 2048], BF16, SL1 + i * 4096) for i in range(2)]
    sb_xpr0 = at("sb_xpr0", [128, 2048], F32, SL1)
    sb_otm2 = [at("sb_otm", [128, 2048], F32, SL2),
               at("sb_otmB", [128, 2048], F32, SL1 + 10624)]
    sb_uact = at("sb_uact", [128, 1024], F32, SL2)
    sb_sact = at("sb_sact", [128, 1024], F32, SL2 + 4096)
    sb_xpr1 = at("sb_xpr1", [128, 2048], F32, SL2)
    sb_xpr = [sb_xpr0, sb_xpr1]
    sb_ao = [at(f"sb_ao{h}", [128, 1024], BF16, B0 + h * 2048) for h in range(NQH)]
    sb_ofm = ([at(f"sb_ofm{i}", [128, 1024], F32, QF0 + i * 4096) for i in range(8)]
              + [at(f"sb_ofm{i + 8}", [128, 1024], F32, KF0 + i * 4096) for i in range(8)])
    sb_ln3T = [at(f"sb_ln3T{c}", [128, 1024], BF16, B0 + c * 2048) for c in range(NKC)]
    sb_h = ([at(f"sb_h{i}", [128, 1024], BF16, A0 + i * 2048) for i in range(32)]
            + [at(f"sb_h{i + 32}", [128, 1024], BF16, QF0 + i * 2048) for i in range(16)]
            + [at(f"sb_h{i + 48}", [128, 1024], BF16, KF0 + i * 2048) for i in range(16)])
    sb_wdb = [at(f"sb_wdb{i}", [128, 4096], BF16, B0 + i * 8192) for i in range(2)]
    sb_dq = [at(f"sb_dq{i}", [128, 256], F32, B0 + 16384 + i * 1024) for i in range(16)]

    S = Sched()
    pst = ExitStack()

    t_ones = S.add("vector", lambda e: e.memset(sb_ones[:], 1.0), inc=True)
    t_n128 = S.add("vector", lambda e: e.memset(sb_n128[:], -128.0), inc=True)
    t_zero = S.add("vector", lambda e: e.memset(sb_zero[:], 0.0), inc=True)
    t_eps = S.add("vector", lambda e: e.memset(sb_eps[:], EPS), inc=True)
    for _par in range(2):
        tw1 = S.add("vector", lambda e, o=sb_ngmx[_par][:, :]:
                    e.reduce_max(o, sb_eps[:, :], axis=AX.X, negate=True),
                    waits=later(t_eps), inc=True)
        tw2 = S.add("scalar", lambda e, o=sb_sume[_par][:, :], b=sb_ngmx[_par][:, :]:
                    e.activation(o, sb_eps[:, :], AF.Exp, bias=b, scale=1.0),
                    waits=later(tw1), inc=True)
        tw3 = S.add("vector", lambda e, o=sb_rsum[_par][:, :], i=sb_sume[_par][:, :]:
                    e.reciprocal(o, i), waits=later(tw2), inc=True)
        S.add("scalar", lambda e, o=sb_lnv[_par][:, :], s=sb_rsum[_par][:, :]:
              e.activation(o, sb_eps[:, :], AF.Copy, bias=0.0, scale=s),
              waits=later(tw3), inc=True)
    t_pm = S.add("vector", lambda e: e.memset(sb_sq[:, :], 1.0), waits=later(t_eps), inc=True)
    tw = t_pm
    for _r in range(6):
        ts1 = S.add("scalar", lambda e: e.activation(sb_sq[:, 0:2048], sb_sq[:, 0:2048],
                                                     AF.Square, bias=sb_zero[:, :], scale=1.0),
                    waits=later(tw), inc=True)
        ts4 = S.add("scalar", lambda e: e.activation(sb_sq[:, 0:512], sb_sq[:, 0:512],
                                                     AF.Silu, bias=sb_zero[:, :], scale=1.0),
                    waits=later(ts1), inc=True)
        ts5 = S.add("scalar", lambda e: e.activation(sb_sq[:, 0:512], sb_sq[:, 0:512],
                                                     AF.Identity, bias=sb_zero[:, :], scale=1.0),
                    waits=later(ts4), inc=True)
        tv1 = S.add("vector", lambda e, o=sb_v2k[_r % 2][:, :]:
                    e.reduce_sum(o, sb_sq[:, :], axis=AX.X), waits=later(ts5), inc=True)
        tv2 = S.add("vector", lambda e, o=sb_ngmx[_r % 2][:, :]:
                    e.reduce_max(o, sb_sq[:, :], axis=AX.X, negate=True),
                    waits=later(tv1), inc=True)
        ts2 = S.add("scalar", lambda e, i=sb_v2k[_r % 2][:, :], o=sb_lnv[_r % 2][:, :]:
                    e.activation(o, i, AF.Ln, bias=sb_eps[:, :], scale=1.0 / H),
                    waits=later(tv1), inc=True)
        ts3 = S.add("scalar", lambda e, i=sb_lnv[_r % 2][:, :], o=sb_rstd[_r % 2][:, :]:
                    e.activation(o, i, AF.Exp, bias=sb_zero[:, :], scale=-0.5),
                    waits=later(ts2), inc=True)
        tv3 = S.add("vector", lambda e, o=sb_rsum[_r % 2][:, :], i=sb_rstd[_r % 2][:, :]:
                    e.reciprocal(o, i), waits=later(ts3), inc=True)
        tv4 = S.add("vector", lambda e, o=sb_nmr[_r % 2][:, :], a=sb_rsum[_r % 2][:, :],
                    b=sb_rstd[_r % 2][:, :]: e.tensor_mul(o, a, b),
                    waits=later(tv3), inc=True)
        tw = later(ts5, tv4, tv2)
    tb1 = S.add("gpsimd", lambda e: e.dma_start(out=d_xp[0:64, :], in_=sb_sq[0:64, :]),
                waits=later(tw), inc=True, inc_by=16)
    tb2 = S.add("gpsimd", lambda e: e.dma_start(out=d_xp[64:128, :], in_=sb_sq[64:128, :]),
                inc=True, inc_by=16)
    t_go = later(tb1, tb2, tw)
    t_zero = later(t_zero, t_go)
    t_eps = later(t_eps, t_go)

    def dma(eng, dst, src, waits=(), inc=True):
        return S.add(eng, lambda e, d=dst, s=src: e.dma_start(out=d, in_=s),
                     waits=waits, inc=inc, inc_by=16)

    t_m0 = dma("sync", sb_m0[:], d_m0[:, :])
    t_mm = dma("sync", sb_mm[:], d_mm[:, :])
    t_idbf = dma("sync", sb_idbf[:], d_idbf[:, :])
    t_id32 = dma("sync", sb_id32[:], d_id32[:, :])
    t_gsl = dma("sync", sb_gs[:], d_gs[:, :])
    t_usl = dma("sync", sb_us[:], d_us[:, :])
    t_dsl = dma("sync", sb_ds[:], d_ds[:, :])
    t_x = [dma("sync", sb_x[t][:], d_x[t * 128:(t + 1) * 128, :]) for t in range(NT)]
    t_xh = dma("sync", sb_xh[:], d_xh[:, :])

    sq_free = {"t": None}

    def ln_tile(x_ap, P, par, dep, out_bf_ap, sqt=None, sqf=None):
        sqt = sb_sq if sqt is None else sqt
        sqf = sq_free if sqf is None else sqf
        a_nsum, a_nm, a_v2k = sb_nsum[par][0:P, :], sb_nm[par][0:P, :], sb_v2k[par][0:P, :]
        a_lnv, a_rstd, a_nmr = sb_lnv[par][0:P, :], sb_rstd[par][0:P, :], sb_nmr[par][0:P, :]
        t0 = S.add("vector", lambda e, x=x_ap, o=a_nsum: e.reduce_sum(o, x, axis=AX.X, negate=True),
                   waits=later(dep), inc=True)
        tm = S.add("scalar", lambda e, o=a_nm, i=a_nsum: e.mul(o, i, 1.0 / H),
                   waits=later(t0), inc=True)
        tb = S.add("vector", lambda e, o=a_nsum, i=a_nm: e.tensor_copy(o, i),
                   waits=later(tm), inc=True)
        tsq = S.add("scalar", lambda e, x=x_ap, b=a_nm, sq=sqt[0:P, :]:
                    e.activation(sq, x, AF.Square, bias=b, scale=1.0),
                    waits=later(sqf["t"], tb), inc=True)
        tvv = S.add("vector", lambda e, o=a_v2k, i=sqt[0:P, :]:
                    e.reduce_sum(o, i, axis=AX.X),
                    waits=later(tsq), inc=True)
        sqf["t"] = tvv
        S.add("scalar", lambda e, o=a_lnv, i=a_v2k, b=sb_eps[0:P, :]:
              e.activation(o, i, AF.Ln, bias=b, scale=1.0 / H),
              waits=later(t_eps, tvv), inc=True)
        t4 = S.add("scalar", lambda e, o=a_rstd, i=a_lnv, b=sb_zero[0:P, :]:
                   e.activation(o, i, AF.Exp, bias=b, scale=-0.5),
                   waits=later(t_zero), inc=True)
        t5 = S.add("vector", lambda e, o=a_nmr, a=a_nm, b=a_rstd: e.tensor_mul(o, a, b),
                   waits=later(t4), inc=True)
        t6 = S.add("scalar", lambda e, o=out_bf_ap, x=x_ap, b=a_nmr, s=a_rstd:
                   e.activation(o, x, AF.Identity, bias=b, scale=s),
                   waits=later(t5), inc=True)
        return t6

    p_tbf = [pst.enter_context(nc.psum_tensor(f"p_tbf{i}", [128, 128], BF16)) for i in range(2)]
    t_ptbf = [None, None]
    t_ln1T_done = []
    t_lnbf_free = [None, None]
    pp_ctr = [0]

    def ln1_iter(tt):
        par = tt % 2
        P = 128 if tt < NT else 11
        x_ap = sb_x[tt][:, :] if tt < NT else sb_xh[:, :]
        dep = later(t_x[tt] if tt < NT else t_xh,
                    (t_x[tt + 1] if tt + 1 < NT else t_xh), t_lnbf_free[par], t_go)
        t_ln = ln_tile(x_ap, P, par, dep, sb_lnbf[par][0:P, :])
        col = tt * 128 if tt < NT else 1024
        for fc in range(NKC):
            pp = pp_ctr[0] % 2
            pp_ctr[0] += 1
            tmm = S.add("tensor",
                        lambda e, o=p_tbf[pp][0:128, 0:P], i=sb_lnbf[par][0:P, fc * 128:(fc + 1) * 128],
                        idn=sb_idbf[0:P, 0:P]:
                        e.transpose(o, i, idn),
                        waits=later(t_ln, t_ptbf[pp], t_idbf), inc=True)
            tev = S.add("vector",
                        lambda e, d=sb_ln1T[fc][:, col:col + P], s=p_tbf[pp][0:128, 0:P]:
                        e.tensor_copy(d, s),
                        waits=later(tmm), inc=True)
            t_ptbf[pp] = tev
            if fc == NKC - 1:
                t_lnbf_free[par] = tmm
            t_ln1T_done.append(tev)

    for tt in range(NT + 1):
        ln1_iter(tt)
    ln1_iter(0)
    ln1_iter(1)
    t_ln1T_all = later(*t_ln1T_done)

    p_mm = [pst.enter_context(nc.psum_tensor(f"p_mm{i}", [128, 512], F32)) for i in range(3)]
    p_mh = pst.enter_context(nc.psum_tensor("p_mh", [128, 16], F32))
    t_pmm = [None, None, None]
    t_pmh = None
    wsrc = [(d_wq, oc, sb_q[oc], 2) for oc in range(16)] + \
           [(d_wk, oc, sb_k[oc], 3) for oc in range(8)] + \
           [(d_wv, oc, sb_v[oc], 3) for oc in range(8)]
    t_wbuf_free = [None, None]
    t_wload = {}

    def wload(i, idx):
        dten, oc, _, _ = wsrc[idx]
        t_wload[idx] = S.add("gpsimd",
                             lambda e, d=sb_wb[i][:, :], s=dten[oc, :, :]: e.dma_start(out=d, in_=s),
                             waits=later(t_wbuf_free[i]), inc=True, inc_by=16)

    wload(0, 0)
    wload(1, 1)
    bank = 0
    for idx, (dten, oc, dst, nseg) in enumerate(wsrc):
        wb = idx % 2
        t_w = t_wload[idx]
        t_grp = None
        for seg in range(nseg):
            if seg < 2:
                ps_ap = p_mm[bank % 3][:, 0:512]
                t_free = t_pmm[bank % 3]
                ncol, rc0 = 512, seg * 512
            else:
                ps_ap = p_mh[:, 0:11]
                t_free = t_pmh
                ncol, rc0 = 11, 1024
            for kc in range(NKC):
                w = later(t_w, t_ln1T_all, t_free) if kc == 0 else ()
                S.add("tensor",
                      lambda e, o=ps_ap, l=sb_wb[wb][:, kc * 128:(kc + 1) * 128],
                      r=sb_ln1T[kc][:, rc0:rc0 + ncol], k=kc:
                      e.matmul(o, l, r, start=(k == 0), stop=(k == NKC - 1)),
                      waits=w, inc=(kc == NKC - 1))
            t_grp = ("tensor", S.cnt["tensor"])
            tev = S.add("vector",
                        lambda e, d=dst[:, rc0:rc0 + ncol], s=ps_ap: e.tensor_copy(d, s),
                        waits=later(t_grp), inc=True)
            if seg < 2:
                t_pmm[bank % 3] = tev
                bank += 1
            else:
                t_pmh = tev
        t_wbuf_free[wb] = t_grp
        if idx + 2 < len(wsrc):
            wload(wb, idx + 2)
    t_wb_free_qkv = list(t_wbuf_free)
    t_qkv_all = later(("vector", S.cnt["vector"]), ("tensor", S.cnt["tensor"]))

    pst.close()
    pst = ExitStack()
    p_ss = [pst.enter_context(nc.psum_tensor(f"p_ss{i}", [128, NKEY], F32)) for i in range(2)]
    p_av = [pst.enter_context(nc.psum_tensor(f"p_av{i}", [128, 128], F32)) for i in range(2)]
    p_vt = [pst.enter_context(nc.psum_tensor(f"p_vt{i}", [128, 128], BF16)) for i in range(2)]
    p_at = [pst.enter_context(nc.psum_tensor(f"p_at{i}", [128, 128], BF16)) for i in range(2)]
    t_pss = [None, None]
    t_pav = [None, None]
    t_pvt = [None, None]
    t_pat = [None, None]
    t_attn_out = []

    _base_it = [(tt, kv) for tt in range(NT) for kv in range(NKV)]
    iters = _base_it[:24] + [(0, 0), (0, 1)] + _base_it[24:]
    NIT = len(iters)
    W = 2 * NIT
    tok_scr = {}
    tok_scr_free = {}
    tok = {k: {} for k in ("sc", "ms", "mx", "ex", "sm", "ln", "rc", "nr", "at", "av")}

    def stage_A(j):
        if j >= NIT:
            return
        tt, kv = iters[j]
        ip = j % 8
        if tt == 0:
            copies = [(0, 1031, 4), (4, 1024, 7), (11, 0, 128)]
        else:
            copies = [(0, 1031, 4), (4, tt * 128 - 7, 7), (11, tt * 128, 128)]
        dep0 = later(t_qkv_all, tok_scr_free.get(j - 8), t_pvt[j % 2])
        tk = tv = None
        for (dcol, scol, n) in copies:
            tk = S.add("vector",
                       lambda e, d=sb_kscr[ip][:, dcol:dcol + n], s=sb_k[kv][:, scol:scol + n]:
                       e.tensor_copy(d, s), waits=dep0, inc=True)
            tv = S.add("vector",
                       lambda e, d=sb_vscr[ip][:, dcol:dcol + n], s=sb_v[kv][:, scol:scol + n]:
                       e.tensor_copy(d, s), inc=True)
            dep0 = ()
        t_scr = later(tk, tv)
        tv1 = S.add("tensor", lambda e, o=p_vt[j % 2][:, :], i=sb_vscr[ip][:, 11:139]:
                    e.transpose(o, i, sb_idbf[:, :]), waits=later(t_scr), inc=True)
        ev1 = S.add("vector", lambda e, d=sb_vt1[ip][:, :], s=p_vt[j % 2][:, :]:
                    e.tensor_copy(d, s), waits=later(tv1), inc=True)
        tv2 = S.add("tensor", lambda e, o=p_vt[j % 2][0:11, :], i=sb_vscr[ip][:, 0:11]:
                    e.transpose(o, i, sb_idbf[:, :]), waits=later(ev1), inc=True)
        ev2 = S.add("vector", lambda e, d=sb_vt2[ip][:, :], s=p_vt[j % 2][0:11, :]:
                    e.tensor_copy(d, s), waits=later(tv2), inc=True)
        t_pvt[j % 2] = ev2
        tok_scr[j] = later(t_scr, ev1, ev2)

    def st_Q(w):
        j, qi = w // 2, w % 2
        tt, kv = iters[j]
        qh = 2 * kv + qi
        tok["sc"][w] = S.add("tensor",
                             lambda e, o=p_ss[w % 2][:, :], l=sb_q[qh][:, tt * 128:(tt + 1) * 128],
                             r=sb_kscr[j % 8][:, :]: e.matmul(o, l, r, start=True, stop=True),
                             waits=later(tok_scr[j], tok["ms"].get(w - 2)), inc=True)

    def st_M(w):
        j = w // 2
        tt, kv = iters[j]
        mask = sb_m0 if tt == 0 else sb_mm
        u4 = w % 4
        tok["ms"][w] = S.add("vector",
                             lambda e, o=sb_ssb[u4][:, :], i=p_ss[w % 2][:, :], m=mask[:, :]:
                             e.scalar_tensor_tensor(o, i, ISQ, m, ALU.mult, ALU.add),
                             waits=later(tok["sc"][w], t_m0, t_mm, tok["ex"].get(w - 4)), inc=True)
        tok["mx"][w] = S.add("vector",
                             lambda e, o=sb_ngmx[u4][:, :], i=sb_ssb[u4][:, :]:
                             e.reduce_max(o, i, axis=AX.X, negate=True),
                             waits=later(tok["ex"].get(w - 4)), inc=True)

    def st_E(w):
        u4, u8 = w % 4, w % 8
        tok["ex"][w] = S.add("scalar",
                             lambda e, o=sb_psb[u8][:, :], i=sb_ssb[u4][:, :], b=sb_ngmx[u4][:, :]:
                             e.activation(o, i, AF.Exp, bias=b, scale=1.0),
                             waits=later(tok["mx"][w], tok["nr"].get(w - 8)), inc=True)

    def st_SM(w):
        u4, u8 = w % 4, w % 8
        tok["sm"][w] = S.add("vector",
                             lambda e, o=sb_sume[u4][:, :], i=sb_psb[u8][:, :]:
                             e.reduce_sum(o, i, axis=AX.X),
                             waits=later(tok["ex"][w], tok["ln"].get(w - 4)), inc=True)

    def st_L(w):
        u4, u8 = w % 4, w % 8
        tok["ln"][w] = S.add("scalar",
                             lambda e, o=sb_rsum[u8][:, :], i=sb_sume[u4][:, :], b=sb_zero[:, :]:
                             e.activation(o, i, AF.Ln, bias=b, scale=1.0),
                             waits=later(tok["sm"][w], t_zero, tok["nr"].get(w - 8)), inc=True)
        S.add("scalar", lambda e, io=sb_rsum[u8][:, :], b=sb_zero[:, :]:
              e.activation(io, io, AF.Exp, bias=b, scale=-1.0), inc=True)
        tok["ln"][w] = ("scalar", S.cnt["scalar"])

    def st_RC(w):
        u8 = w % 8
        tok["rc"][w] = S.add("vector",
                             lambda e, i=sb_rsum[u8][:, :]: e.tensor_copy(sb_trash[:, :], i),
                             waits=later(tok["ln"][w]), inc=True)

    def st_N(w):
        u4, u8 = w % 4, w % 8
        tok["nr"][w] = S.add("scalar",
                             lambda e, o=sb_abf[u4][:, :], i=sb_psb[u8][:, :], s=sb_rsum[u8][:, :]:
                             e.activation(o, i, AF.Copy, bias=0.0, scale=s),
                             waits=later(tok["rc"][w], tok["at"].get(w - 4)), inc=True)

    def st_T(w):
        u4 = w % 4
        ta1 = S.add("tensor", lambda e, o=p_at[w % 2][:, :], i=sb_abf[u4][:, 11:139]:
                    e.transpose(o, i, sb_idbf[:, :]),
                    waits=later(tok["nr"][w], tok["av"].get(w - 2)), inc=True)
        ea1 = S.add("vector", lambda e, d=sb_at1[u4][:, :], s=p_at[w % 2][:, :]:
                    e.tensor_copy(d, s), waits=later(ta1), inc=True)
        ta2 = S.add("tensor", lambda e, o=p_at[w % 2][0:11, :], i=sb_abf[u4][:, 0:11]:
                    e.transpose(o, i, sb_idbf[:, :]), waits=later(ea1), inc=True)
        ea2 = S.add("vector", lambda e, d=sb_at2[u4][:, :], s=p_at[w % 2][0:11, :]:
                    e.tensor_copy(d, s), waits=later(ta2), inc=True)
        tok["at"][w] = later(ea1, ea2)

    def st_V(w):
        j, qi = w // 2, w % 2
        tt, kv = iters[j]
        qh = 2 * kv + qi
        u4, ip = w % 4, j % 8
        S.add("tensor",
              lambda e, o=p_av[w % 2][:, :], l=sb_vt1[ip][:, :], r=sb_at1[u4][:, :]:
              e.matmul(o, l, r, start=True, stop=False),
              waits=later(tok_scr[j], tok["at"][w], t_pav[w % 2]))
        tav = S.add("tensor",
                    lambda e, o=p_av[w % 2][:, :], l=sb_vt2[ip][:, :], r=sb_at2[u4][:, :]:
                    e.matmul(o, l, r, start=False, stop=True), inc=True)
        eav = S.add("vector",
                    lambda e, d=sb_ao[qh][:, tt * 128:(tt + 1) * 128], s=p_av[w % 2][:, :]:
                    e.tensor_copy(d, s), waits=later(tav), inc=True)
        t_pav[w % 2] = eav
        t_attn_out.append(eav)
        tok["av"][w] = tav
        if qi == 1:
            tok_scr_free[j] = tav

    STAGES = [st_Q, st_M, st_E, st_SM, st_L, st_RC, st_N, st_T, st_V]
    NSTG = len(STAGES)
    stage_A(0)
    stage_A(1)
    for w in range(W + NSTG - 1):
        if w % 2 == 0 and w // 2 + 2 < NIT + 2:
            stage_A(w // 2 + 2)
        for si, fn in enumerate(STAGES):
            ww = w - si
            if 0 <= ww < W:
                fn(ww)
    t_attn_all = later(*t_attn_out)

    pst.close()
    pst = ExitStack()
    p_wo = [pst.enter_context(nc.psum_tensor(f"p_wo{i}", [128, 512], F32)) for i in range(2)]
    p_t32 = [pst.enter_context(nc.psum_tensor(f"p_t32{i}", [128, 128], F32)) for i in range(2)]
    p_tb3 = [pst.enter_context(nc.psum_tensor(f"p_tb3{i}", [128, 128], BF16)) for i in range(2)]
    p_cs = [pst.enter_context(nc.psum_tensor(f"p_cs{i}", [1, 512], F32)) for i in range(2)]
    t_pwo = [None, None]
    t_pt32 = [None, None]
    t_ptb3 = [None, None]
    t_wo = [None, None]
    t_ofm = []
    bank = 0

    def wo_load(i, oc, free_tok):
        t_wo[i] = S.add("gpsimd",
                        lambda e, d=sb_wb[i][:, :], s=d_wo[oc, :, :]: e.dma_start(out=d, in_=s),
                        waits=later(free_tok), inc=True, inc_by=16)

    wo_load(0, 0, t_wb_free_qkv[0])
    wo_load(1, 1, t_wb_free_qkv[1])
    for oc in range(16):
        wb = oc % 2
        t_w = t_wo[wb]
        t_grp = None
        for th in range(2):
            ps = p_wo[bank % 2]
            for kc in range(NKC):
                w = later(t_w, t_attn_all, t_pwo[bank % 2]) if kc == 0 else ()
                S.add("tensor",
                      lambda e, o=ps[:, :], l=sb_wb[wb][:, kc * 128:(kc + 1) * 128],
                      r=sb_ao[kc][:, th * 512:(th + 1) * 512], k=kc:
                      e.matmul(o, l, r, start=(k == 0), stop=(k == NKC - 1)),
                      waits=w, inc=(kc == NKC - 1))
            t_grp = ("tensor", S.cnt["tensor"])
            tev = S.add("vector",
                        lambda e, d=sb_ofm[oc][:, th * 512:(th + 1) * 512], s=ps[:, :]:
                        e.tensor_copy(d, s), waits=later(t_grp), inc=True)
            t_pwo[bank % 2] = tev
            t_ofm.append(tev)
            bank += 1
        if oc + 2 < 16:
            wo_load(wb, oc + 2, t_grp)
    t_ofm_all = later(*t_ofm)

    t_otm_free = [None, None]
    t_park = []
    t_ln3T_done = []
    sq_freeB = {"t": None}
    sq4 = [(sb_sq, sq_free), (sb_sqB, sq_freeB)]
    for tt in range(NT):
        par = tt % 2
        sqt4, sqf4 = sq4[par]
        c = slice(par, par + 1)
        sb_otm = sb_otm2[par]
        evs = []
        for fc in range(NKC):
            pp = (tt * NKC + fc) % 2
            tmm = S.add("tensor",
                        lambda e, o=p_t32[pp][:, :], i=sb_ofm[fc][:, tt * 128:(tt + 1) * 128]:
                        e.transpose(o, i, sb_id32[:, :]),
                        waits=later(t_ofm_all, t_pt32[pp], t_otm_free[par], t_id32), inc=True)
            tev = S.add("vector",
                        lambda e, d=sb_otm[:, fc * 128:(fc + 1) * 128], s=p_t32[pp][:, :]:
                        e.tensor_copy(d, s), waits=later(tmm), inc=True)
            t_pt32[pp] = tev
            evs.append(tev)
        t_otm = later(*evs)
        a_nsum, a_nm, a_v2k = sb_nsum[par][:, :], sb_nm[par][:, :], sb_v2k[par][:, :]
        a_lnv, a_rstd, a_nmr = sb_lnv[par][:, :], sb_rstd[par][:, :], sb_nmr[par][:, :]
        t0 = S.add("vector", lambda e, o=a_nsum, i=sb_otm[:, :]: e.reduce_sum(o, i, axis=AX.X, negate=True),
                   waits=later(t_otm), inc=True)
        tm = S.add("scalar", lambda e, o=a_nm, i=a_nsum: e.mul(o, i, 1.0 / H),
                   waits=later(t0), inc=True)
        tb = S.add("vector", lambda e, o=a_nsum, i=a_nm: e.tensor_copy(o, i),
                   waits=later(tm), inc=True)
        tsq = S.add("scalar", lambda e, b=a_nm, i=sb_otm[:, :], sq=sqt4[:, :]:
                    e.activation(sq, i, AF.Square, bias=b, scale=1.0),
                    waits=later(sqf4["t"], tb), inc=True)
        tvv = S.add("vector", lambda e, o=a_v2k, i=sqt4[:, :]: e.reduce_sum(o, i, axis=AX.X),
                    waits=later(tsq), inc=True)
        sqf4["t"] = tvv
        S.add("scalar", lambda e, o=a_lnv, i=a_v2k, b=sb_eps[:, :]:
              e.activation(o, i, AF.Ln, bias=b, scale=1.0 / H),
              waits=later(t_eps, tvv), inc=True)
        t4 = S.add("scalar", lambda e, o=a_rstd, i=a_lnv, b=sb_zero[:, :]:
                   e.activation(o, i, AF.Exp, bias=b, scale=-0.5),
                   waits=later(t_zero), inc=True)
        t5 = S.add("vector", lambda e, o=a_nmr, a=a_nm, b=a_rstd: e.tensor_mul(o, a, b),
                   waits=later(t4), inc=True)
        t6 = S.add("scalar", lambda e, b=a_nmr, s=a_rstd, io=sb_otm[:, :]:
                   e.activation(io, io, AF.Identity, bias=b, scale=s),
                   waits=later(t5), inc=True)
        t7 = S.add("vector", lambda e, x=sb_x[tt][:, :], o=sb_otm[:, :]: e.tensor_add(x, x, o),
                   waits=later(t6), inc=True)
        t_otm_free[par] = t7
        tp = S.add("sync", lambda e, d=d_xp[tt * 128:(tt + 1) * 128, :], s=sb_x[tt][:, :]:
                   e.dma_start(out=d, in_=s), waits=later(t7), inc=True, inc_by=16)
        t_park.append(tp)
        t_ln = ln_tile(sb_x[tt][:, :], 128, par, later(t7, t_lnbf_free[par], t_attn_all),
                       sb_lnbf[par][:, :], sqt=sqt4, sqf=sqf4)
        for fc in range(NKC):
            pp = (tt * NKC + fc) % 2
            tmm = S.add("tensor",
                        lambda e, o=p_tb3[pp][:, :], i=sb_lnbf[par][:, fc * 128:(fc + 1) * 128]:
                        e.transpose(o, i, sb_idbf[:, :]),
                        waits=later(t_ln, t_ptb3[pp]), inc=True)
            tev = S.add("vector",
                        lambda e, d=sb_ln3T[fc][:, tt * 128:(tt + 1) * 128], s=p_tb3[pp][:, :]:
                        e.tensor_copy(d, s), waits=later(tmm), inc=True)
            t_ptb3[pp] = tev
            t_ln3T_done.append(tev)
            if fc == NKC - 1:
                t_lnbf_free[par] = tmm
    t_ln3T_all = later(*t_ln3T_done)
    t_park_all = later(*t_park)

    t_csum = []
    for th in range(2):
        for kc in range(NKC):
            w = later(t_ln3T_all, t_ones) if kc == 0 else ()
            S.add("tensor",
                  lambda e, o=p_cs[th][0:1, :], l=sb_ones[:, 0:1],
                  r=sb_ln3T[kc][:, th * 512:(th + 1) * 512], k=kc:
                  e.matmul(o, l, r, start=(k == 0), stop=(k == NKC - 1)),
                  waits=w, inc=(kc == NKC - 1))
        t_grp = ("tensor", S.cnt["tensor"])
        tev = S.add("vector",
                    lambda e, d=sb_csum[0:1, th * 512:(th + 1) * 512], s=p_cs[th][0:1, :]:
                    e.tensor_copy(d, s), waits=later(t_grp), inc=True)
        t_csum.append(tev)
    t_csum_all = later(*t_csum)

    pst.close()
    pst = ExitStack()
    p_u = [pst.enter_context(nc.psum_tensor(f"p_u{i}", [128, 512], F32)) for i in range(4)]
    p_g = [pst.enter_context(nc.psum_tensor(f"p_g{i}", [128, 512], F32)) for i in range(4)]
    t_pu = [None] * 4
    t_pg = [None] * 4
    t_uact_free = None
    t_sact_free = None
    t_h = [None] * NIC
    t_wgl = {}
    t_wul = {}

    def gu_load(tab, bufs, dten, ic, i, free_tok):
        tab[ic] = S.add("gpsimd",
                        lambda e, d=bufs[i][:, :], s=dten[ic, :, :]: e.dma_start(out=d, in_=s),
                        waits=later(free_tok), inc=True, inc_by=16)

    gu_load(t_wgl, sb_wgb, d_wg, 0, 0, later(t_ofm_all))
    gu_load(t_wgl, sb_wgb, d_wg, 1, 1, later(t_ofm_all))
    gu_load(t_wul, sb_wub, d_wu, 0, 0, later(t_attn_all, t_ln3T_all))
    gu_load(t_wul, sb_wub, d_wu, 1, 1, later(t_attn_all, t_ln3T_all))
    for ic in range(NIC):
        wb = ic % 2
        bk = ic % 2
        for th in range(2):
            ps = p_u[2 * bk + th]
            for kc in range(NKC):
                w = later(t_wul[ic], t_ln3T_all, t_csum_all, t_n128,
                          t_pu[2 * bk + th], t_park_all) if kc == 0 else ()
                S.add("tensor",
                      lambda e, o=ps[:, :], l=sb_wub[wb][:, kc * 128:(kc + 1) * 128],
                      r=sb_ln3T[kc][:, th * 512:(th + 1) * 512], k=kc:
                      e.matmul(o, l, r, start=(k == 0), stop=False), waits=w)
            S.add("tensor",
                  lambda e, o=ps[:, :], l=sb_n128[0:1, :], r=sb_csum[0:1, th * 512:(th + 1) * 512]:
                  e.matmul(o, l, r, start=False, stop=True), inc=True)
        t_ugrp = ("tensor", S.cnt["tensor"])
        if ic + 2 < NIC:
            gu_load(t_wul, sb_wub, d_wu, ic + 2, wb, t_ugrp)
        ues = []
        for th in range(2):
            tev = S.add("scalar",
                        lambda e, d=sb_uact[:, th * 512:(th + 1) * 512], s=p_u[2 * bk + th][:, :],
                        sc=sb_us[:, ic:ic + 1]:
                        e.activation(d, s, AF.Copy, bias=0.0, scale=sc),
                        waits=later(t_ugrp, t_uact_free, t_usl), inc=True)
            t_pu[2 * bk + th] = tev
            ues.append(tev)
        for th in range(2):
            ps = p_g[2 * bk + th]
            for kc in range(NKC):
                w = later(t_wgl[ic], t_pg[2 * bk + th]) if kc == 0 else ()
                S.add("tensor",
                      lambda e, o=ps[:, :], l=sb_wgb[wb][:, kc * 128:(kc + 1) * 128],
                      r=sb_ln3T[kc][:, th * 512:(th + 1) * 512], k=kc:
                      e.matmul(o, l, r, start=(k == 0), stop=False), waits=w)
            S.add("tensor",
                  lambda e, o=ps[:, :], l=sb_n128[0:1, :], r=sb_csum[0:1, th * 512:(th + 1) * 512]:
                  e.matmul(o, l, r, start=False, stop=True), inc=True)
        t_ggrp = ("tensor", S.cnt["tensor"])
        if ic + 2 < NIC:
            gu_load(t_wgl, sb_wgb, d_wg, ic + 2, wb, t_ggrp)
        for th in range(2):
            tsv = S.add("scalar",
                        lambda e, d=sb_sact[:, th * 512:(th + 1) * 512], s=p_g[2 * bk + th][:, :],
                        sc=sb_gs[:, ic:ic + 1]:
                        e.activation(d, s, AF.Silu, bias=sb_zero[:, :], scale=sc),
                        waits=later(t_ggrp, t_sact_free, t_gsl, t_zero), inc=True)
            t_pg[2 * bk + th] = tsv
            thm = S.add("vector",
                        lambda e, d=sb_h[ic][:, th * 512:(th + 1) * 512],
                        a=sb_sact[:, th * 512:(th + 1) * 512], b=sb_uact[:, th * 512:(th + 1) * 512]:
                        e.tensor_mul(d, a, b),
                        waits=later(tsv, ues[th], t_park_all), inc=True)
            if th == 1:
                t_h[ic] = thm
                t_uact_free = thm
                t_sact_free = thm
    t_h_all = later(*t_h)
    t_gu_tensor_done = ("tensor", S.cnt["tensor"])

    pst.close()
    pst = ExitStack()
    p_hs = [pst.enter_context(nc.psum_tensor(f"p_hs{i}", [1, 512], F32)) for i in range(2)]
    p_d = [pst.enter_context(nc.psum_tensor(f"p_d{i}", [128, 256], F32)) for i in range(2)]
    p_t32b = [pst.enter_context(nc.psum_tensor(f"p_t32b{i}", [128, 128], F32)) for i in range(2)]

    t_hsum = []
    for th in range(2):
        for ic in range(NIC):
            S.add("tensor",
                  lambda e, o=p_hs[th][0:1, :], l=sb_ones[:, 0:1],
                  r=sb_h[ic][:, th * 512:(th + 1) * 512], k=ic:
                  e.matmul(o, l, r, start=(k == 0), stop=(k == NIC - 1)),
                  waits=(later(t_h_all) if ic == 0 else ()), inc=(ic == NIC - 1))
        t_grp = ("tensor", S.cnt["tensor"])
        tev = S.add("vector",
                    lambda e, d=sb_hsum[0:1, th * 512:(th + 1) * 512], s=p_hs[th][0:1, :]:
                    e.tensor_copy(d, s), waits=later(t_grp), inc=True)
        t_hsum.append(tev)
    t_hsum_all = later(*t_hsum)
    t_pd = [None, None]
    t_pt32b = [None, None]
    t_dq_free = [None] * 16
    t_xpr_free = [None, None]
    t_dtm_free = [None, None]
    t_out = []
    wd_seq = [(tq, hf, half) for tq in range(4) for hf in range(16) for half in range(2)]
    t_wdl = {}

    def wd_load(seq_i, free_tok):
        tq_, hf_, half_ = wd_seq[seq_i]
        t_wdl[seq_i] = S.add("gpsimd",
                             lambda e, d=sb_wdb[seq_i % 2][:, :],
                             s=d_wd[hf_, :, half_ * 4096:(half_ + 1) * 4096]:
                             e.dma_start(out=d, in_=s),
                             waits=later(free_tok), inc=True, inc_by=16)

    wd_load(0, later(t_gu_tensor_done))
    wd_load(1, later(t_gu_tensor_done))
    for tq in range(4):
        for hf in range(16):
            seq_i = tq * 32 + hf * 2
            pb = hf % 2
            ps = p_d[pb]
            for half in range(2):
                t_w = t_wdl[seq_i + half]
                for j in range(32):
                    ic = half * 32 + j
                    if half == 0 and j == 0:
                        w = later(t_w, t_h_all, t_hsum_all, t_pd[pb])
                    elif j == 0:
                        w = later(t_w)
                    else:
                        w = ()
                    S.add("tensor",
                          lambda e, o=ps[:, :], l=sb_wdb[(seq_i + half) % 2][:, j * 128:(j + 1) * 128],
                          r=sb_h[ic][:, tq * 256:(tq + 1) * 256], st=(half == 0 and j == 0):
                          e.matmul(o, l, r, start=st, stop=False),
                          waits=w, inc=(j == 31))
                t_half = ("tensor", S.cnt["tensor"])
                if seq_i + half + 2 < len(wd_seq):
                    wd_load(seq_i + half + 2, t_half)
            S.add("tensor",
                  lambda e, o=ps[:, :], l=sb_n128[0:1, :], r=sb_hsum[0:1, tq * 256:(tq + 1) * 256]:
                  e.matmul(o, l, r, start=False, stop=True), inc=True)
            t_grp = ("tensor", S.cnt["tensor"])
            tev = S.add("scalar",
                        lambda e, d=sb_dq[hf][:, :], s=ps[:, :], sc=sb_ds[:, hf:hf + 1]:
                        e.activation(d, s, AF.Copy, bias=0.0, scale=sc),
                        waits=later(t_grp, t_dq_free[hf], t_dsl), inc=True)
            t_pd[pb] = tev
            t_dq_free[hf] = None
        t_dq_all = ("scalar", S.cnt["scalar"])
        for sub in range(2):
            tt = tq * 2 + sub
            par = tt % 2
            c = slice(par, par + 1)
            sb_dtm = sb_dtm2[par]
            txp = S.add("sync",
                        lambda e, d=sb_xpr[par][:, :], s=d_xp[tt * 128:(tt + 1) * 128, :]:
                        e.dma_start(out=d, in_=s),
                        waits=later(t_park_all, t_xpr_free[par], t_h_all), inc=True, inc_by=16)
            evs = []
            for fc in range(NKC):
                pp = (tt * NKC + fc) % 2
                tmm = S.add("tensor",
                            lambda e, o=p_t32b[pp][:, :], i=sb_dq[fc][:, sub * 128:(sub + 1) * 128]:
                            e.transpose(o, i, sb_id32[:, :]),
                            waits=later(t_dq_all, t_pt32b[pp], t_dtm_free[par]), inc=True)
                tev = S.add("vector",
                            lambda e, d=sb_dtm[:, fc * 128:(fc + 1) * 128], s=p_t32b[pp][:, :]:
                            e.tensor_copy(d, s), waits=later(tmm), inc=True)
                t_pt32b[pp] = tev
                evs.append(tev)
                if sub == 1:
                    t_dq_free[fc] = tmm
            t_dtm = later(*evs)
            a_nsum, a_nm, a_v2k = sb_nsum[par][:, :], sb_nm[par][:, :], sb_v2k[par][:, :]
            a_lnv, a_rstd, a_nmr = sb_lnv[par][:, :], sb_rstd[par][:, :], sb_nmr[par][:, :]
            t0 = S.add("vector", lambda e, o=a_nsum, i=sb_dtm[:, :]: e.reduce_sum(o, i, axis=AX.X, negate=True),
                  waits=later(t_dtm), inc=True)
            tm = S.add("scalar", lambda e, o=a_nm, i=a_nsum: e.mul(o, i, 1.0 / H),
                  waits=later(t0), inc=True)
            tb = S.add("vector", lambda e, o=a_nsum, i=a_nm: e.tensor_copy(o, i),
                       waits=later(tm), inc=True)
            tsq = S.add("scalar", lambda e, b=a_nm, i=sb_dtm[:, :]:
                        e.activation(sb_sq[:, :], i, AF.Square, bias=b, scale=1.0),
                        waits=later(sq_free["t"], tb), inc=True)
            tvv = S.add("vector", lambda e, o=a_v2k: e.reduce_sum(o, sb_sq[:, :], axis=AX.X),
                        waits=later(tsq), inc=True)
            sq_free["t"] = tvv
            S.add("scalar", lambda e, o=a_lnv, i=a_v2k, b=sb_eps[:, :]:
                  e.activation(o, i, AF.Ln, bias=b, scale=1.0 / H),
                  waits=later(t_eps, tvv), inc=True)
            t4 = S.add("scalar", lambda e, o=a_rstd, i=a_lnv, b=sb_zero[:, :]:
                       e.activation(o, i, AF.Exp, bias=b, scale=-0.5),
                       waits=later(t_zero), inc=True)
            t5 = S.add("vector", lambda e, o=a_nmr, a=a_nm, b=a_rstd: e.tensor_mul(o, a, b),
                       waits=later(t4), inc=True)
            t6 = S.add("scalar", lambda e, b=a_nmr, s=a_rstd, io=sb_dtm[:, :]:
                       e.activation(io, io, AF.Identity, bias=b, scale=s),
                       waits=later(t5), inc=True)
            t7 = S.add("vector", lambda e, xp=sb_xpr[par][:, :], o=sb_dtm[:, :]:
                       e.tensor_add(xp, xp, o), waits=later(t6, txp), inc=True)
            t_dtm_free[par] = t7
            to = S.add("sync", lambda e, d=d_out[tt * 128:(tt + 1) * 128, :], s=sb_xpr[par][:, :]:
                       e.dma_start(out=d, in_=s), waits=later(t7), inc=True, inc_by=16)
            t_xpr_free[par] = to
            t_out.append(to)

    final = S.cnt["sync"]
    S.add("sync", lambda e: e.nop(), waits=[("sync", final)])

    sem_stack = ExitStack()
    sems = {e: sem_stack.enter_context(nc.semaphore(f"sem_{e}")) for e in ENGINES}
    S.emit(nc, sems)
    sem_stack.close()
    pst.close()
    return nc


def _host_layout(inputs):
    x = np.ascontiguousarray(np.asarray(inputs["L0_attn_x"]).reshape(S_TOT, H))
    Wq = np.asarray(inputs["L0_attn_Wq"])
    Wk = np.asarray(inputs["L0_attn_Wk"])
    Wv = np.asarray(inputs["L0_attn_Wv"])
    Wo = np.asarray(inputs["L0_attn_Wo"])
    gw = np.asarray(inputs["L0_ffn_gate_w"])
    uw = np.asarray(inputs["L0_ffn_up_w"])
    dw = np.asarray(inputs["L0_ffn_down_w"])
    gsc = np.asarray(inputs["L0_ffn_gate_scale"])
    usc = np.asarray(inputs["L0_ffn_up_scale"])
    dsc = np.asarray(inputs["L0_ffn_down_scale"])

    def wtile(W, noc):
        return np.ascontiguousarray(
            W.reshape(noc, 128, NKC, 128).transpose(0, 3, 2, 1).reshape(noc, 128, 2048))

    wq, wk, wv, wo = wtile(Wq, 16), wtile(Wk, 8), wtile(Wv, 8), wtile(Wo, 16)

    def u8_low(a):
        return np.ascontiguousarray(a.view(np.uint8).reshape(a.shape[0], a.shape[1], 4)[:, :, 0])

    g8, u8, dn8 = u8_low(gw), u8_low(uw), u8_low(dw)
    wg8 = np.ascontiguousarray(g8.reshape(NIC, 128, NKC, 128).transpose(0, 3, 2, 1).reshape(NIC, 128, 2048))
    wu8 = np.ascontiguousarray(u8.reshape(NIC, 128, NKC, 128).transpose(0, 3, 2, 1).reshape(NIC, 128, 2048))
    wd8 = np.ascontiguousarray(dn8.reshape(16, 128, NIC, 128).transpose(0, 3, 2, 1).reshape(16, 128, 8192))

    gs = np.ascontiguousarray(gsc.reshape(NIC, 128).T)
    us = np.ascontiguousarray(usc.reshape(NIC, 128).T)
    ds = np.ascontiguousarray(dsc.reshape(16, 128).T)

    r = np.arange(128)[:, None]
    cc = np.arange(NKEY)[None, :]
    maskm = np.full((128, NKEY), NEG, dtype=np.float32)
    maskm[:, 0:4] = 0.0
    band = (cc >= r + 4) & (cc <= r + 11) & (cc >= 4)
    maskm[band] = 0.0
    mask0 = np.full((128, NKEY), NEG, dtype=np.float32)
    j = cc - 11
    ok0 = (cc >= 11) & (j <= r) & ((j < 4) | (r - j < 8))
    mask0[ok0] = 0.0

    in_maps = []
    for core in range(NCORE):
        s0 = core * T
        xh = np.empty((11, H), dtype=np.float32)
        xh[0:7] = x[0:7] if core == 0 else x[s0 - 7:s0]
        xh[7:11] = x[0:4]
        in_maps.append({
            "x": np.ascontiguousarray(x[s0:s0 + T]),
            "xh": xh,
            "wq": wq, "wk": wk, "wv": wv, "wo": wo,
            "wg8": wg8, "wu8": wu8, "wd8": wd8,
            "gs": gs, "us": us, "ds": ds,
            "mask0": np.ascontiguousarray(mask0 if core == 0 else maskm),
            "maskm": maskm,
        })
    return in_maps


def kernel(**inputs):
    in_maps = _host_layout(inputs)
    nc = build_nc()
    res = run_bass_kernel_spmd(nc, in_maps, core_ids=list(range(NCORE)))
    out = np.concatenate([r["out"] for r in res.results], axis=0)
    return out.reshape(1, S_TOT, H).astype(np.float32, copy=False)
